# version 11
# baseline (speedup 1.0000x reference)
import numpy as np
import ml_dtypes
from contextlib import ExitStack

import concourse.bass as bass
import concourse.mybir as mybir
from concourse.bass_utils import run_bass_kernel_spmd

BFnp = ml_dtypes.bfloat16
F32 = mybir.dt.float32
BF16 = mybir.dt.bfloat16
AF = mybir.ActivationFunctionType
ALU = mybir.AluOpType
AX = mybir.AxisListType

NCORES = 8
BC = 2
L = 2048
CM = 1024
NE = 8
TC = BC * L
NT = TC // 128
NTS = L // 128
LN0125 = float(np.log(0.125))

DEBUG_X1 = False
DEBUG_ATT = False


def _perm_qk():
    perm = np.zeros(CM, dtype=np.int64)
    for cc in range(8):
        base = cc * 128
        for half in range(2):
            h = 2 * cc + half
            for j in range(32):
                perm[base + half * 64 + j] = h * 64 + 2 * j
                perm[base + half * 64 + 32 + j] = h * 64 + 2 * j + 1
    return perm


def _dperm(p):
    return 2 * (p % 32) + ((p % 64) // 32)


def prep_weights(inp):
    w = {}
    perm = _perm_qk()
    w_qkv = inp["L0_attn_w_qkv"]
    b_qkv = inp["L0_attn_b_qkv"]
    wqp = w_qkv[0:1024][perm]
    wkp = w_qkv[1024:2048][perm]
    wvp = w_qkv[2048:3072]
    w["wk"] = np.ascontiguousarray(
        wkp.reshape(8, 128, 8, 128).transpose(3, 0, 2, 1).reshape(128, 8192))
    w["wv"] = np.ascontiguousarray(
        wvp.T.reshape(8, 128, 1024).transpose(1, 0, 2).reshape(128, 8192))
    w["wq"] = np.ascontiguousarray(
        wqp.reshape(8, 128, 8, 128).transpose(0, 3, 2, 1).reshape(8, 128, 1024))
    w["wout"] = np.ascontiguousarray(
        inp["L0_attn_w_out"].T.reshape(8, 128, 1024).transpose(1, 0, 2).reshape(128, 8192))
    w["bqk"] = np.ascontiguousarray(
        np.stack([b_qkv[0:1024][perm].reshape(8, 128).T,
                  b_qkv[1024:2048][perm].reshape(8, 128).T], axis=-1).reshape(128, 16))
    w["bv"] = inp["L0_attn_b_qkv"][2048:3072].reshape(1, 1024)
    w["bout"] = inp["L0_attn_b_out"].reshape(1, 1024)
    w["brt"] = np.ascontiguousarray(
        np.pad(inp["L0_ffn_routing_bias"], (0, 1016)).reshape(1, 1024))
    def gmat(g):
        m = np.zeros((16, 1024), dtype=np.float32)
        for cc in range(8):
            for p in range(128):
                h = 2 * cc + p // 64
                m[h, cc * 128 + p] = g[h, _dperm(p)]
        return m
    w["gbq"] = gmat(inp["L0_attn_q_gamma"])
    w["gbk"] = gmat(inp["L0_attn_k_gamma"])
    w["wrt"] = np.ascontiguousarray(
        inp["L0_ffn_router_w"].T.reshape(8, 128, 8).transpose(1, 0, 2).reshape(128, 64))

    def wgl(m):
        return np.ascontiguousarray(
            m.reshape(16, 128, 8, 128).transpose(3, 0, 2, 1).reshape(128, 16384))

    def wdl(m):
        return np.ascontiguousarray(
            m.T.reshape(16, 128, 1024).transpose(1, 0, 2).reshape(128, 16384))
    w["sg"] = wgl(inp["L0_ffn_sg_w"])
    w["su"] = wgl(inp["L0_ffn_su_w"])
    w["sd"] = wdl(inp["L0_ffn_sd_w"])
    w["wg"] = np.stack([wgl(inp["L0_ffn_wg"][e]) for e in range(NE)])
    w["wu"] = np.stack([wgl(inp["L0_ffn_wu"][e]) for e in range(NE)])
    w["wd"] = np.stack([wdl(inp["L0_ffn_wd"][e]) for e in range(NE)])
    return w


def _consts():
    c = {}
    inv = 1.0 / (10000.0 ** (np.arange(32, dtype=np.float64) / 32))
    ang = np.arange(L, dtype=np.float64)[:, None] * inv[None, :]
    cos = np.cos(ang).T
    sin = np.sin(ang).T
    c["cs1"] = np.concatenate([cos, cos, cos, cos], 0).astype(BFnp)
    c["cs2"] = np.concatenate([-sin, sin, -sin, sin], 0).astype(BFnp)
    c["idf"] = np.eye(128, dtype=np.float32)
    c["on1"] = np.ones((1, 128), dtype=BFnp)
    blk = np.zeros((128, 128), dtype=BFnp)
    for cc in range(8):
        for p in range(128):
            blk[p, cc * 16 + 2 * cc + p // 64] = 1
    c["blk"] = blk
    hs = np.zeros((16, 1024), dtype=BFnp)
    for oc in range(8):
        for p in range(128):
            hs[2 * oc + p // 64, oc * 128 + p] = 1
    c["hs"] = hs
    return c


def build_program():
    nc = bass.Bass("TRN2", target_bir_lowering=False)

    x_d = nc.dram_tensor("x", [TC, CM], F32, kind="ExternalInput").ap()
    out_d = nc.dram_tensor("out", [TC, CM], F32, kind="ExternalOutput").ap()
    wk_d = nc.dram_tensor("wk", [128, 8192], F32, kind="ExternalInput").ap()
    wv_d = nc.dram_tensor("wv", [128, 8192], F32, kind="ExternalInput").ap()
    wq_d = nc.dram_tensor("wq", [8, 128, 1024], F32, kind="ExternalInput").ap()
    wout_d = nc.dram_tensor("wout", [128, 8192], F32, kind="ExternalInput").ap()
    bqk_d = nc.dram_tensor("bqk", [128, 16], F32, kind="ExternalInput").ap()
    bv_d = nc.dram_tensor("bv", [1, 1024], F32, kind="ExternalInput").ap()
    bout_d = nc.dram_tensor("bout", [1, 1024], F32, kind="ExternalInput").ap()
    brt_d = nc.dram_tensor("brt", [1, 1024], F32, kind="ExternalInput").ap()
    gbq_d = nc.dram_tensor("gbq", [16, 1024], F32, kind="ExternalInput").ap()
    gbk_d = nc.dram_tensor("gbk", [16, 1024], F32, kind="ExternalInput").ap()
    wrt_d = nc.dram_tensor("wrt", [128, 64], F32, kind="ExternalInput").ap()
    sg_d = nc.dram_tensor("sg", [128, 16384], F32, kind="ExternalInput").ap()
    su_d = nc.dram_tensor("su", [128, 16384], F32, kind="ExternalInput").ap()
    sd_d = nc.dram_tensor("sd", [128, 16384], F32, kind="ExternalInput").ap()
    wg_d = nc.dram_tensor("wg", [NE, 128, 16384], F32, kind="ExternalInput").ap()
    wu_d = nc.dram_tensor("wu", [NE, 128, 16384], F32, kind="ExternalInput").ap()
    wd_d = nc.dram_tensor("wd", [NE, 128, 16384], F32, kind="ExternalInput").ap()

    x1_d = nc.dram_tensor("x1sp", [TC, CM], BF16,
                          kind="ExternalOutput" if DEBUG_X1 else "Internal").ap()
    ysh_d = nc.dram_tensor("ysh", [TC, CM], BF16,
                           kind="ExternalOutput" if DEBUG_ATT else "Internal").ap()
    if DEBUG_ATT:
        dbgz_d = nc.dram_tensor("dbgz", [16, 4096], BF16, kind="ExternalOutput").ap()
        dbgo_d = nc.dram_tensor("dbgo", [128, 16384], BF16, kind="ExternalOutput").ap()
        dbgq_d = nc.dram_tensor("dbgq", [128, 4096], BF16, kind="ExternalOutput").ap()
        dbgk_d = nc.dram_tensor("dbgk", [128, 16384], BF16, kind="ExternalOutput").ap()
    ye_d = nc.dram_tensor("ye", [NE * TC, CM], BF16,
                          kind="ExternalOutput" if DEBUG_ATT else "Internal").ap()

    ct = _consts()
    cs1_d = nc.inline_tensor(ct["cs1"], "cs1c").ap()
    cs2_d = nc.inline_tensor(ct["cs2"], "cs2c").ap()
    idf_d = nc.inline_tensor(ct["idf"], "idfc").ap()
    on1_d = nc.inline_tensor(ct["on1"], "on1c").ap()
    blk_d = nc.inline_tensor(ct["blk"], "blkc").ap()
    hs_d = nc.inline_tensor(ct["hs"], "hsc").ap()

    ctx = ExitStack()
    sb = lambda n, sh, dt: ctx.enter_context(nc.sbuf_tensor(n, sh, dt))
    A = sb("Ar", [128, 32768], BF16)
    Bv = sb("Bv", [128, 16640], BF16)
    Cc = sb("Cc", [128, 16384], BF16)
    Dd = sb("Dd", [128, 16384], BF16)
    HH = sb("HH", [128, 8192], BF16)
    YS = sb("YS", [128, 2048], BF16)
    XT = sb("XT", [128, 2048], BF16)
    G = sb("G", [128, 1024], F32)
    TMP = sb("TMP", [128, 1536], BF16)
    CS1 = sb("CS1", [128, 2048], BF16)
    CS2 = sb("CS2", [128, 2048], BF16)
    IDF = sb("IDF", [128, 128], F32)
    ON1 = sb("ON1", [1, 128], BF16)
    BLK = sb("BLK", [128, 128], BF16)
    GB = sb("GB", [16, 1024], BF16)
    BQK = sb("BQK", [128, 16], F32)
    BRW = sb("BRW", [1, 1024], BF16)
    WRT = sb("WRT", [128, 64], BF16)
    RST = sb("RST", [16, 512], BF16)
    STT = sb("STT", [128, 8], F32)
    PR = sb("PR", [128, 24], F32)
    CB = sb("CB", [128, 4], F32)
    WTM = sb("WTM", [128, 256], F32)

    psA = ctx.enter_context(nc.psum_tensor("psA", [128, 2048], F32))
    psB = ctx.enter_context(nc.psum_tensor("psB", [128, 2048], F32))

    sems = {}
    semnames = (["ld", "pe", "ac", "dv", "po", "wb", "xa", "xb", "qa", "qb",
                 "sw", "z", "ob", "x1a", "x1b", "pa", "pb", "ya", "yb",
                 "fl", "out"]
                + [f"wj{j}" for j in range(16)] + [f"wd{j}" for j in range(16)])
    for s in semnames:
        sems[s] = ctx.enter_context(nc.semaphore("s_" + s))
    N = {k: 0 for k in sems}
    OPS = {e: [] for e in ("tensor", "vector", "scalar", "gpsimd", "sync")}

    def emit(eng, build, waits=(), incs=()):
        w = tuple(waits)
        i = tuple(incs)

        def fn(e, _b=build, _w=w, _i=i):
            for s, v in _w:
                if v > 0:
                    e.wait_ge(sems[s], v)
            inst = _b(e)
            for s, a in _i:
                inst.then_inc(sems[s], a)
        OPS[eng].append(fn)
        for s, a in i:
            N[s] += a

    def wait_only(eng, waits):
        w = tuple(waits)

        def fn(e, _w=w):
            for s, v in _w:
                if v > 0:
                    e.wait_ge(sems[s], v)
        OPS[eng].append(fn)

    cons = {}

    for dst, src in ((CS1, cs1_d), (CS2, cs2_d), (IDF, idf_d), (ON1, on1_d),
                     (BLK, blk_d), (BQK, bqk_d)):
        emit("sync", lambda e, dst=dst, src=src: e.dma_start(dst[:, :], src),
             incs=[("ld", 16)])
    p0_ld = N["ld"]
    emit("gpsimd", lambda e: e.dma_start(WRT[:, :], wrt_d), incs=[("wb", 16)])
    p0_wrt = N["wb"]
    emit("vector",
         lambda e: e.memset(Bv[:, :].rearrange(
             "p (m c) -> p m c", c=65)[:, :, 64:65], 1.0),
         incs=[("dv", 1)])
    for col, val in ((0, 1e-5), (1, 1e-6), (2, -8.0), (3, LN0125)):
        emit("vector", lambda e, col=col, val=val: e.memset(
            CB[:, col:col + 1], val), incs=[("dv", 1)])
    vones_dv = N["dv"]

    def ln_tile(src_ap, pe_war_key, extra_waits=()):
        emit("scalar", lambda e: e.activation(
            G[:, :], src_ap, AF.Copy, accum_out=STT[:, 0:1]),
            waits=list(extra_waits) + [("ac", N["ac"]),
                                       ("pe", cons.get(pe_war_key, 0)),
                                       ("dv", cons.get("lnstt", 0))],
            incs=[("ac", 1)])
        emit("scalar", lambda e: e.activation(
            G[:, :], src_ap, AF.Square, accum_out=STT[:, 1:2]),
            waits=[("ac", N["ac"])], incs=[("ac", 1)])
        ac_st = N["ac"]
        emit("vector", lambda e: e.tensor_scalar(
            STT[:, 2:3], STT[:, 0:1], 1.0 / 1024, None, ALU.mult),
            waits=[("ac", ac_st), ("dv", N["dv"])], incs=[("dv", 1)])
        emit("vector", lambda e: e.tensor_tensor(
            STT[:, 3:4], STT[:, 2:3], STT[:, 2:3], ALU.mult),
            waits=[("dv", N["dv"])], incs=[("dv", 1)])
        emit("vector", lambda e: e.scalar_tensor_tensor(
            STT[:, 4:5], STT[:, 1:2], 1.0 / 1024, STT[:, 3:4],
            ALU.mult, ALU.subtract),
            waits=[("dv", N["dv"])], incs=[("dv", 1)])
        dv_var = N["dv"]
        emit("scalar", lambda e: e.activation(
            STT[:, 5:6], STT[:, 4:5], AF.Ln, bias=CB[:, 0:1]),
            waits=[("dv", dv_var), ("ac", N["ac"])], incs=[("ac", 1)])
        emit("scalar", lambda e: e.activation(
            STT[:, 6:7], STT[:, 5:6], AF.Exp, scale=-0.5),
            waits=[("ac", N["ac"])], incs=[("ac", 1)])
        ac_r = N["ac"]
        emit("vector", lambda e: e.tensor_scalar(
            G[:, :], src_ap, STT[:, 2:3], STT[:, 6:7], ALU.subtract, ALU.mult),
            waits=[("ac", ac_r), ("dv", N["dv"])], incs=[("dv", 1)])
        cons["lnstt"] = N["dv"]
        return N["dv"]

    def transpose_to(dst_base_fn, dv_h):
        for cc in range(8):
            wt = [("dv", dv_h)]
            if cc == 0:
                wt.append(("dv", cons.get("psAtr", 0)))
                wt.append(("ld", p0_ld))
            emit("tensor", lambda e, cc=cc: e.transpose(
                psA[:, cc * 128:(cc + 1) * 128],
                G[:, cc * 128:(cc + 1) * 128], IDF[:, :]),
                waits=(wt if cc == 0 else []),
                incs=([("pe", 1)] if cc == 7 else []))
        cons["htr"] = N["pe"]
        pe_tr = N["pe"]
        for cc in range(8):
            emit("vector", lambda e, cc=cc: e.tensor_copy(
                dst_base_fn(cc), psA[:, cc * 128:(cc + 1) * 128]),
                waits=([("pe", pe_tr), ("dv", N["dv"])] if cc == 0 else []),
                incs=([("dv", 1)] if cc == 7 else []))
        cons["psAtr"] = N["dv"]
        return N["dv"]

    def rope_rms(kind, psrc, ps_free_key, bias_col, cs_col, out_col, out_buf,
                 blk_col, sum_start, sum_stop, exp_bias):
        emit("scalar", lambda e, psrc=psrc: e.activation(
            TMP[:, 0:512], psrc, AF.Identity, bias=BQK[:, bias_col:bias_col + 1]),
            waits=[("dv", cons.get("tmprope", 0)), ("ac", N["ac"]),
                   ("pe", cons.get("sqmm", 0)), ("ld", p0_ld),
                   ("po", cons.get("swfree", 0))],
            incs=[("ac", 1)])
        cons[ps_free_key] = N["ac"]
        ac_raw = N["ac"]
        for qr in range(4):
            dq = qr ^ 1
            emit("sync", lambda e, qr=qr, dq=dq: e.dma_start(
                TMP[dq * 32:(dq + 1) * 32, 512:1024],
                TMP[qr * 32:(qr + 1) * 32, 0:512]),
                waits=([("ac", ac_raw), ("po", cons.get("swfree", 0))]
                       if qr == 0 else []),
                incs=[("sw", 16)])
        ld_sw = N["sw"]
        emit("vector", lambda e, cs_col=cs_col: e.tensor_tensor(
            TMP[:, 1024:1536], TMP[:, 0:512], CS1[:, cs_col:cs_col + 512], ALU.mult),
            waits=[("ac", ac_raw), ("dv", N["dv"])], incs=[("dv", 1)])
        dv_t1 = N["dv"]
        emit("gpsimd", lambda e, cs_col=cs_col: e.tensor_tensor(
            TMP[:, 0:512], TMP[:, 512:1024], CS2[:, cs_col:cs_col + 512], ALU.mult),
            waits=[("sw", ld_sw), ("dv", dv_t1), ("po", N["po"])],
            incs=[("po", 1)])
        cons["swfree"] = N["po"]
        po_t2 = N["po"]
        oc = out_col
        emit("vector", lambda e, oc=oc, ob=out_buf: e.tensor_tensor(
            ob[:, oc:oc + 512], TMP[:, 1024:1536], TMP[:, 0:512], ALU.add),
            waits=[("po", po_t2), ("dv", N["dv"])], incs=[("dv", 1)])
        cons["tmprope"] = N["dv"]
        po_r = N["dv"]
        emit("scalar", lambda e, oc=oc, ob=out_buf: e.activation(
            TMP[:, 0:512], ob[:, oc:oc + 512], AF.Square),
            waits=[("dv", po_r), ("ac", N["ac"]), ("pe", cons.get("sqmm", 0))],
            incs=[("ac", 1)])
        ac_sq = N["ac"]
        wt = [("ac", ac_sq)]
        if sum_start:
            wt += [("dv", cons.get("ssfree_dv", 0)), ("ac", cons.get("ssfree_ac", 0)),
                   ("dv", cons.get(("psBv", 0), 0)), ("dv", cons.get("psBo", 0))]
        emit("tensor", lambda e, blk_col=blk_col: e.matmul(
            psB[0:16, 0:512], BLK[:, blk_col:blk_col + 16], TMP[:, 0:512],
            start=sum_start, stop=sum_stop),
            waits=wt, incs=([("pe", 1)] if sum_stop else []))
        cons["sqmm"] = N["pe"]
        if not sum_stop:
            return None
        pe_ss = N["pe"]
        emit("scalar", lambda e: e.activation(
            psB[0:16, 512:1024], psB[0:16, 0:512], AF.Ln, bias=CB[0:16, 1:2],
            scale=1.0 / 64),
            waits=[("pe", pe_ss), ("ac", N["ac"])], incs=[("ac", 1)])
        if exp_bias == 0.0:
            emit("scalar", lambda e: e.activation(
                RST[0:16, 0:512], psB[0:16, 512:1024], AF.Exp, scale=-0.5),
                waits=[("ac", N["ac"]), ("pe", cons.get("rstfree", 0))],
                incs=[("ac", 1)])
        else:
            emit("scalar", lambda e: e.activation(
                RST[0:16, 0:512], psB[0:16, 512:1024], AF.Exp, scale=-0.5,
                bias=CB[0:16, 3:4]),
                waits=[("ac", N["ac"])], incs=[("ac", 1)])
        cons["ssfree_ac"] = N["ac"]
        return N["ac"]

    def rms_apply(ac_rstd, gb_off, out_col, out_buf):
        emit("tensor", lambda e, gb_off=gb_off: e.matmul(
            psB[:, 1024:1536], GB[0:16, gb_off:gb_off + 128], RST[0:16, 0:512],
            start=True, stop=True),
            waits=[("ac", ac_rstd), ("dv", cons.get("bcfree", 0)),
                   ("dv", cons.get(("psBv", 1), 0))],
            incs=[("pe", 1)])
        cons["rstfree"] = N["pe"]
        pe_bc = N["pe"]
        emit("vector", lambda e, oc=out_col, ob=out_buf: e.tensor_tensor(
            ob[:, oc:oc + 512], ob[:, oc:oc + 512], psB[:, 1024:1536], ALU.mult),
            waits=[("pe", pe_bc), ("dv", N["dv"])], incs=[("dv", 1)])
        cons["bcfree"] = N["dv"]
        cons["ssfree_dv"] = N["dv"]
        return N["dv"]

    for s in range(BC):
        pre_wt = [("pe", N["pe"]), ("dv", N["dv"])]
        emit("gpsimd", lambda e: e.dma_start(Cc[:, 0:8192], wk_d),
             waits=pre_wt, incs=[("wb", 16)])
        emit("gpsimd", lambda e: e.dma_start(Cc[:, 8192:16384], wv_d),
             incs=[("wb", 16)])
        emit("gpsimd", lambda e: e.dma_start(BRW[:, :], bv_d), incs=[("wb", 16)])
        emit("gpsimd", lambda e: e.dma_start(GB[:, :], gbk_d), incs=[("wb", 16)])
        wl_kv = N["wb"]
        emit("vector",
             lambda e: e.memset(Bv[:, :].rearrange(
                 "p (m c) -> p m c", c=65)[:, :, 64:65], 1.0),
             waits=[("pe", N["pe"]), ("dv", N["dv"])], incs=[("dv", 1)])
        vones_dv = N["dv"]

        h_dv = {}
        for g in range(4):
            for ti in range(4):
                tt = g * 4 + ti
                tok = s * L + tt * 128
                gs = tt % 2
                xsem = "xa" if gs == 0 else "xb"
                emit("gpsimd", lambda e, tok=tok, gs=gs: e.dma_start(
                    XT[:, gs * 1024:(gs + 1) * 1024], x_d[tok:tok + 128, :]),
                    waits=[("dv", cons.get(("XT", gs), 0)),
                           ("pe", N["pe"] if (s == 1 and tt == 0) else 0)],
                    incs=[(xsem, 16)])
                dv_h = ln_tile(XT[:, gs * 1024:(gs + 1) * 1024], "htr",
                               extra_waits=[(xsem, N[xsem])])
                cons[("XT", gs)] = dv_h
                transpose_to(
                    (lambda cc, tt=tt: A[:, cc * 2048 + tt * 128:
                                         cc * 2048 + tt * 128 + 128]), dv_h)
                h_dv[tt] = N["dv"]
            h_rdy = h_dv[g * 4 + 3]
            for sub in range(4):
                tt = g * 4 + sub
                slot = sub % 2
                wt = [("dv", h_rdy), ("wb", wl_kv), ("ld", p0_ld),
                      ("dv", cons.get(("psBv", slot), 0)),
                      ("ac", cons.get("ssfree_ac", 0)),
                      ("dv", cons.get("bcfree", 0)),
                      ("dv", vones_dv),
                      ("dv", cons.get("psBo", 0)),
                      ("dv", cons.get("bk6", 0)),
                      ("ac", cons.get("psq", 0))]
                first = True
                for cc in range(8):
                    for hf in range(2):
                        emit("tensor", lambda e, cc=cc, hf=hf, slot=slot, tt=tt: e.matmul(
                            psB[:, slot * 1024 + hf * 512: slot * 1024 + hf * 512 + 512],
                            A[:, cc * 2048 + tt * 128: cc * 2048 + tt * 128 + 128],
                            Cc[:, 8192 + cc * 1024 + hf * 512:
                               8192 + cc * 1024 + hf * 512 + 512],
                            start=(cc == 0), stop=False),
                            waits=(wt if first else []))
                        first = False
                for hf in range(2):
                    emit("tensor", lambda e, hf=hf, slot=slot: e.matmul(
                        psB[:, slot * 1024 + hf * 512: slot * 1024 + hf * 512 + 512],
                        ON1[0:1, :], BRW[0:1, hf * 512:(hf + 1) * 512],
                        start=False, stop=True),
                        incs=([("pe", 1)] if hf == 1 else []))
                pe_v = N["pe"]
                emit("vector", lambda e, slot=slot, tt=tt: e.tensor_copy(
                    Bv[:, tt * 1040:(tt + 1) * 1040].rearrange(
                        "p (h c) -> p h c", c=65)[:, :, 0:64],
                    psB[:, slot * 1024: slot * 1024 + 1024].rearrange(
                        "p (h c) -> p h c", c=64)),
                    waits=[("pe", pe_v), ("dv", N["dv"])], incs=[("dv", 1)])
                cons[("psBv", slot)] = N["dv"]
            lc = g * 512
            for kc in range(8):
                slot = kc % 2
                ps0 = 1024 + slot * 512
                wt = [("dv", h_rdy), ("ac", cons.get(("psAk", slot), 0)),
                      ("wb", wl_kv)]
                for cc in range(8):
                    emit("tensor", lambda e, cc=cc, kc=kc, ps0=ps0, lc=lc: e.matmul(
                        psA[:, ps0:ps0 + 512],
                        Cc[:, kc * 1024 + cc * 128: kc * 1024 + (cc + 1) * 128],
                        A[:, cc * 2048 + lc: cc * 2048 + lc + 512],
                        start=(cc == 0), stop=(cc == 7)),
                        waits=(wt if cc == 0 else []),
                        incs=([("pe", 1)] if cc == 7 else []))
                pe_k = N["pe"]
                wait_only("scalar", [("pe", pe_k)])
                ac_rstd = rope_rms(
                    "k", psA[:, ps0:ps0 + 512], ("psAk", slot),
                    bias_col=2 * kc + 1, cs_col=lc,
                    out_col=16384 + kc * 2048 + lc, out_buf=A,
                    blk_col=kc * 16, sum_start=(kc == 0), sum_stop=(kc == 7),
                    exp_bias=0.0)
            for kc in range(8):
                rms_apply(ac_rstd, kc * 128, 16384 + kc * 2048 + lc, A)
        k_done = N["dv"]

        emit("gpsimd", lambda e: e.dma_start(GB[:, :], gbq_d),
             waits=[("pe", N["pe"])], incs=[("wb", 16)])
        wl_gbq = N["wb"]

        for hp in range(8):
            wqs = hp % 2
            qsem = "qa" if wqs == 0 else "qb"
            emit("gpsimd", lambda e, hp=hp, wqs=wqs: e.dma_start(
                Dd[:, 4096 + wqs * 1024: 4096 + (wqs + 1) * 1024], wq_d[hp]),
                waits=[("pe", cons.get(("wq", wqs), 0))], incs=[(qsem, 16)])
            wl_wq = (qsem, N[qsem])
            q_mark = {}

            def q_mm(lt, hp=hp, wqs=wqs, wl_wq=wl_wq):
                wt = [wl_wq, ("dv", k_done),
                      ("ac", cons.get("psq", 0)), ("wb", wl_gbq),
                      ("dv", cons.get(("psBv", 1), 0))]
                for cc in range(8):
                    emit("tensor", lambda e, cc=cc, lt=lt, wqs=wqs: e.matmul(
                        psB[:, 1536:2048],
                        Dd[:, 4096 + wqs * 1024 + cc * 128:
                           4096 + wqs * 1024 + (cc + 1) * 128],
                        A[:, cc * 2048 + lt * 512: cc * 2048 + lt * 512 + 512],
                        start=(cc == 0), stop=(cc == 7)),
                        waits=(wt if cc == 0 else []),
                        incs=([("pe", 1)] if cc == 7 else []))
                cons[("wq", wqs)] = N["pe"]
                pe_q = N["pe"]
                emit("scalar", lambda e, hp=hp: e.activation(
                    TMP[:, 0:512], psB[:, 1536:2048], AF.Identity,
                    bias=BQK[:, 2 * hp:2 * hp + 1]),
                    waits=[("pe", pe_q), ("dv", cons.get("tmprope", 0)),
                           ("ac", N["ac"]), ("pe", cons.get("sqmm", 0)),
                           ("po", cons.get("swfree", 0)), ("ld", p0_ld)],
                    incs=[("ac", 1)])
                cons["psq"] = N["ac"]
                ac_raw = N["ac"]
                for qr in range(4):
                    dq = qr ^ 1
                    emit("sync", lambda e, qr=qr, dq=dq: e.dma_start(
                        TMP[dq * 32:(dq + 1) * 32, 512:1024],
                        TMP[qr * 32:(qr + 1) * 32, 0:512]),
                        waits=([("ac", ac_raw), ("po", cons.get("swfree", 0))]
                               if qr == 0 else []),
                        incs=[("sw", 16)])
                ld_sw = N["sw"]
                cs_col = lt * 512
                emit("vector", lambda e, cs_col=cs_col: e.tensor_tensor(
                    TMP[:, 1024:1536], TMP[:, 0:512], CS1[:, cs_col:cs_col + 512],
                    ALU.mult),
                    waits=[("ac", ac_raw), ("dv", N["dv"])], incs=[("dv", 1)])
                dv_t1 = N["dv"]
                emit("gpsimd", lambda e, cs_col=cs_col: e.tensor_tensor(
                    TMP[:, 0:512], TMP[:, 512:1024], CS2[:, cs_col:cs_col + 512],
                    ALU.mult),
                    waits=[("sw", ld_sw), ("dv", dv_t1), ("po", N["po"])],
                    incs=[("po", 1)])
                cons["swfree"] = N["po"]
                po_t2 = N["po"]
                qcol = lt * 512
                emit("vector", lambda e, qcol=qcol: e.tensor_tensor(
                    Dd[:, qcol:qcol + 512], TMP[:, 1024:1536], TMP[:, 0:512],
                    ALU.add),
                    waits=[("po", po_t2), ("dv", N["dv"])], incs=[("dv", 1)])
                cons["tmprope"] = N["dv"]
                q_mark[("rope", lt)] = N["dv"]

            def q_fin(lt, hp=hp):
                qcol = lt * 512
                emit("scalar", lambda e, qcol=qcol: e.activation(
                    TMP[:, 0:512], Dd[:, qcol:qcol + 512], AF.Square),
                    waits=[("dv", q_mark[("rope", lt)]), ("ac", N["ac"]),
                           ("pe", cons.get("sqmm", 0))],
                    incs=[("ac", 1)])
                q_mark[("sq", lt)] = N["ac"]
                emit("tensor", lambda e, hp=hp: e.matmul(
                    psB[0:16, 1024:1536], BLK[:, hp * 16:(hp + 1) * 16],
                    TMP[:, 0:512], start=True, stop=True),
                    waits=[("ac", q_mark[("sq", lt)]),
                           ("dv", cons.get("bk6", 0)),
                           ("ac", cons.get("bk6a", 0)),
                           ("dv", cons.get(("psBv", 1), 0)),
                           ("dv", cons.get("bcfree", 0))],
                    incs=[("pe", 1)])
                cons["sqmm"] = N["pe"]
                pe_ss = N["pe"]
                emit("scalar", lambda e: e.activation(
                    G[0:16, 0:512], psB[0:16, 1024:1536], AF.Ln, bias=CB[0:16, 1:2],
                    scale=1.0 / 64),
                    waits=[("pe", pe_ss), ("ac", N["ac"])], incs=[("ac", 1)])
                emit("scalar", lambda e: e.activation(
                    RST[0:16, 0:512], G[0:16, 0:512], AF.Exp, scale=-0.5,
                    bias=CB[0:16, 3:4]),
                    waits=[("ac", N["ac"]), ("pe", cons.get("rstfree", 0))],
                    incs=[("ac", 1)])
                ac_rstd = N["ac"]
                cons["bk6a"] = N["ac"]
                emit("tensor", lambda e, hp=hp: e.matmul(
                    psB[:, 1024:1536], GB[0:16, hp * 128:(hp + 1) * 128],
                    RST[0:16, 0:512], start=True, stop=True),
                    waits=[("ac", ac_rstd)], incs=[("pe", 1)])
                pe_bc = N["pe"]
                cons["rstfree"] = N["pe"]
                emit("vector", lambda e, qcol=lt * 512: e.tensor_tensor(
                    Dd[:, qcol:qcol + 512], Dd[:, qcol:qcol + 512],
                    psB[:, 1024:1536], ALU.mult),
                    waits=[("pe", pe_bc), ("dv", N["dv"])], incs=[("dv", 1)])
                cons["bk6"] = N["dv"]
                q_mark[lt] = N["dv"]

            def m_loop(lt, hp=hp):
                for mt in range(16):
                    half = mt % 2
                    sc0 = half * 1024
                    wt = [("dv", q_mark[lt]),
                          ("ac", cons.get(("expf", half), 0))]
                    emit("tensor", lambda e, hp=hp, mt=mt, lt=lt, sc0=sc0: e.matmul(
                        psA[:, sc0:sc0 + 512],
                        A[0:64, 16384 + hp * 2048 + mt * 128:
                          16384 + hp * 2048 + mt * 128 + 128],
                        Dd[0:64, lt * 512: lt * 512 + 512],
                        start=True, stop=True, tile_position=(0, 0)),
                        waits=wt)
                    emit("tensor", lambda e, hp=hp, mt=mt, lt=lt, sc0=sc0: e.matmul(
                        psA[:, sc0 + 512:sc0 + 1024],
                        A[64:128, 16384 + hp * 2048 + mt * 128:
                          16384 + hp * 2048 + mt * 128 + 128],
                        Dd[64:128, lt * 512: lt * 512 + 512],
                        start=True, stop=True, tile_position=(64, 0)),
                        incs=[("pe", 1)])
                    emit("scalar", lambda e, sc0=sc0, half=half: e.activation(
                        Dd[:, 2048 + half * 1024: 2048 + half * 1024 + 1024],
                        psA[:, sc0:sc0 + 1024], AF.Exp, bias=CB[:, 2:3]),
                        waits=[("pe", N["pe"]), ("pe", cons.get(("Pw", half), 0)),
                               ("ac", N["ac"])],
                        incs=[("ac", 1)])
                    cons[("expf", half)] = N["ac"]
                    ac_exp = N["ac"]
                    wt2 = [("ac", ac_exp)]
                    if mt == 0:
                        wt2.append(("dv", cons.get("psBo", 0)))
                        wt2.append(("dv", cons.get(("psBv", 0), 0)))
                        wt2.append(("ac", cons.get("ssfree_ac", 0)))
                    for hd in range(2):
                        emit("tensor", lambda e, hp=hp, mt=mt, hd=hd, half=half: e.matmul(
                            psB[0:65, hd * 512: hd * 512 + 512],
                            Bv[:, mt * 1040 + (2 * hp + hd) * 65:
                               mt * 1040 + (2 * hp + hd) * 65 + 65],
                            Dd[:, 2048 + half * 1024 + hd * 512:
                               2048 + half * 1024 + hd * 512 + 512],
                            start=(mt == 0), stop=(mt == 15)),
                            waits=(wt2 if hd == 0 else []),
                            incs=([("pe", 1)] if hd == 1 else []))
                    cons[("Pw", half)] = N["pe"]
                pe_av = N["pe"]
                emit("vector", lambda e: e.tensor_copy(
                    Dd[64:65, 10240:10752], psB[64:65, 0:512]),
                    waits=[("pe", pe_av), ("z", N["z"]),
                           ("dv", N["dv"])],
                    incs=[("dv", 1)])
                emit("vector", lambda e: e.tensor_copy(
                    Dd[64:65, 10752:11264], psB[64:65, 512:1024]),
                    waits=[("dv", N["dv"])], incs=[("dv", 1)])
                dv_z = N["dv"]
                emit("sync", lambda e, hp=hp, lt=lt: e.dma_start(
                    Dd[2 * hp:2 * hp + 2, 6144 + lt * 512:6144 + lt * 512 + 512],
                    Dd[64:65, 10240:11264]),
                    waits=[("dv", dv_z)], incs=[("z", 16)])
                emit("vector", lambda e, hp=hp, lt=lt: e.tensor_copy(
                    Cc[0:64, hp * 2048 + lt * 512: hp * 2048 + lt * 512 + 512],
                    psB[0:64, 0:512]),
                    waits=[("pe", pe_av), ("dv", N["dv"])], incs=[("dv", 1)])
                emit("vector", lambda e: e.tensor_copy(
                    YS[0:64, 0:512], psB[0:64, 512:1024]),
                    waits=[("pe", pe_av), ("ob", N["ob"]),
                           ("dv", N["dv"]), ("x1a", N["x1a"])],
                    incs=[("dv", 1)])
                dv_ob = N["dv"]
                emit("sync", lambda e, hp=hp, lt=lt: e.dma_start(
                    Cc[64:128, hp * 2048 + lt * 512: hp * 2048 + lt * 512 + 512],
                    YS[0:64, 0:512]),
                    waits=[("dv", dv_ob)], incs=[("ob", 16)])
                cons["psBo"] = N["dv"]

            q_mm(0)
            q_fin(0)
            q_mm(1)
            for lt in range(4):
                m_loop(lt)
                if lt + 1 < 4:
                    q_fin(lt + 1)
                if lt + 2 < 4:
                    q_mm(lt + 2)
        z_ld = N["z"]

        emit("vector", lambda e: e.reciprocal(
            Dd[0:16, 8192:10240], Dd[0:16, 6144:8192]),
            waits=[("z", z_ld), ("dv", N["dv"]), ("pe", N["pe"])],
            incs=[("dv", 1)])
        dv_rz = N["dv"]
        emit("gpsimd", lambda e: e.dma_start(Bv[:, 0:8192], wout_d),
             waits=[("pe", N["pe"])], incs=[("wb", 16)])
        emit("gpsimd", lambda e: e.dma_start(BRW[:, :], bout_d), incs=[("wb", 16)])
        emit("gpsimd", lambda e: e.dma_start(GB[:, :], hs_d),
             waits=[("pe", N["pe"])], incs=[("wb", 16)])
        wl_oc = N["wb"]
        ld_ohop = N["ob"]
        for oc in range(8):
            for lh in range(2):
                for hf in range(2):
                    emit("tensor", lambda e, oc=oc, lh=lh, hf=hf: e.matmul(
                        psA[:, hf * 512: hf * 512 + 512],
                        GB[0:16, oc * 128:(oc + 1) * 128],
                        Dd[0:16, 8192 + lh * 1024 + hf * 512:
                           8192 + lh * 1024 + hf * 512 + 512],
                        start=True, stop=True),
                        waits=([("dv", dv_rz), ("wb", wl_oc),
                                ("dv", cons.get("onorm", 0))] if hf == 0 else []),
                        incs=([("pe", 1)] if hf == 1 else []))
                pe_bc = N["pe"]
                emit("vector", lambda e, oc=oc, lh=lh: e.tensor_tensor(
                    Cc[:, oc * 2048 + lh * 1024: oc * 2048 + lh * 1024 + 1024],
                    Cc[:, oc * 2048 + lh * 1024: oc * 2048 + lh * 1024 + 1024],
                    psA[:, 0:1024], ALU.mult),
                    waits=[("pe", pe_bc), ("dv", N["dv"]), ("ob", ld_ohop)],
                    incs=[("dv", 1)])
                cons["onorm"] = N["dv"]
        dv_on = N["dv"]
        if DEBUG_ATT and s == 1:
            emit("sync", lambda e: e.dma_start(dbgz_d, Dd[0:16, 10240:14336]),
                 waits=[("dv", dv_on)], incs=[("ld", 16)])
            emit("sync", lambda e: e.dma_start(dbgo_d, Cc[:, :]),
                 waits=[("dv", dv_on)], incs=[("ld", 16)])
            emit("sync", lambda e: e.dma_start(dbgq_d, Dd[:, 0:4096]),
                 waits=[("dv", dv_on)], incs=[("ld", 16)])
            emit("sync", lambda e: e.dma_start(dbgk_d, A[:, 16384:32768]),
                 waits=[("dv", dv_on)], incs=[("ld", 16)])
        for tt in range(NTS):
            tok = s * L + tt * 128
            slot = tt % 2
            gs = tt % 2
            xsem = "xa" if gs == 0 else "xb"
            emit("gpsimd", lambda e, tok=tok, gs=gs: e.dma_start(
                XT[:, gs * 1024:(gs + 1) * 1024], x_d[tok:tok + 128, :]),
                waits=[("dv", cons.get(("XT", gs), 0))], incs=[(xsem, 16)])
            ld_x0 = (xsem, N[xsem])
            wt = [("dv", dv_on), ("wb", wl_oc), ("ld", p0_ld),
                  ("dv", cons.get(("psBp", slot), 0))]
            first = True
            for oc in range(8):
                for hf in range(2):
                    emit("tensor", lambda e, oc=oc, hf=hf, tt=tt, slot=slot: e.matmul(
                        psB[:, slot * 1024 + hf * 512: slot * 1024 + hf * 512 + 512],
                        Cc[:, oc * 2048 + tt * 128: oc * 2048 + tt * 128 + 128],
                        Bv[:, oc * 1024 + hf * 512: oc * 1024 + hf * 512 + 512],
                        start=(oc == 0), stop=False),
                        waits=(wt if first else []))
                    first = False
            for hf in range(2):
                emit("tensor", lambda e, hf=hf, slot=slot: e.matmul(
                    psB[:, slot * 1024 + hf * 512: slot * 1024 + hf * 512 + 512],
                    ON1[0:1, :], BRW[0:1, hf * 512:(hf + 1) * 512],
                    start=False, stop=True),
                    incs=([("pe", 1)] if hf == 1 else []))
            pe_pj = N["pe"]
            emit("vector", lambda e, slot=slot, gs=gs: e.tensor_tensor(
                YS[:, slot * 1024:(slot + 1) * 1024],
                XT[:, gs * 1024:(gs + 1) * 1024],
                psB[:, slot * 1024: slot * 1024 + 1024], ALU.add),
                waits=[("pe", pe_pj), ld_x0,
                       ("x1a" if slot == 0 else "x1b",
                        N["x1a" if slot == 0 else "x1b"]), ("dv", N["dv"])],
                incs=[("dv", 1)])
            cons[("XT", gs)] = N["dv"]
            cons[("psBp", slot)] = N["dv"]
            dv_x1 = N["dv"]
            x1sem = "x1a" if slot == 0 else "x1b"
            emit("sync", lambda e, tok=tok, slot=slot: e.dma_start(
                x1_d[tok:tok + 128, :], YS[:, slot * 1024:(slot + 1) * 1024]),
                waits=[("dv", dv_x1)], incs=[(x1sem, 16)])
    x1_st = (N["x1a"], N["x1b"])

    emit("gpsimd", lambda e: e.dma_start(BRW[:, :], brt_d),
         waits=[("pe", N["pe"])], incs=[("wb", 16)])

    h2_marks = {}
    wtm_marks = {}
    for tt in range(NT):
        tok = tt * 128
        gs = tt % 2
        psem = "pa" if gs == 0 else "pb"
        emit("sync", lambda e, tok=tok, gs=gs: e.dma_start(
            YS[:, gs * 1024:(gs + 1) * 1024], x1_d[tok:tok + 128, :]),
            waits=[("x1a", x1_st[0]), ("x1b", x1_st[1]),
                   ("dv", cons.get(("YS2", gs), 0))],
            incs=[(psem, 16)])
        dv_h = ln_tile(YS[:, gs * 1024:(gs + 1) * 1024], "htr",
                       extra_waits=[(psem, N[psem])])
        cons[("YS2", gs)] = dv_h
        dv_h2 = transpose_to(
            (lambda cc, tt=tt: A[:, cc * 4096 + tt * 128:
                                 cc * 4096 + tt * 128 + 128]), dv_h)
        h2_marks[tt] = dv_h2
        wt = [("dv", dv_h2), ("ac", cons.get("sig", 0)),
              ("ld", p0_ld), ("wb", N["wb"])]
        for cc in range(8):
            emit("tensor", lambda e, cc=cc, tt=tt: e.matmul(
                psB[:, 0:8],
                A[:, cc * 4096 + tt * 128: cc * 4096 + tt * 128 + 128],
                WRT[:, cc * 8:(cc + 1) * 8], start=(cc == 0), stop=False),
                waits=(wt if cc == 0 else []))
        emit("tensor", lambda e: e.matmul(
            psB[:, 0:8], ON1[0:1, :], BRW[0:1, 0:8], start=False, stop=True),
            incs=[("pe", 1)])
        pe_lg = N["pe"]
        emit("scalar", lambda e: e.activation(
            PR[:, 0:8], psB[:, 0:8], AF.Sigmoid),
            waits=[("pe", pe_lg), ("dv", cons.get("top2", 0)), ("ac", N["ac"])],
            incs=[("ac", 1)])
        cons["sig"] = N["ac"]
        ac_pr = N["ac"]

        def t2(b, w0=None):
            emit("vector", b,
                 waits=[("dv", N["dv"])] + ([w0] if w0 else []), incs=[("dv", 1)])
        t2(lambda e: e.reduce_max(PR[:, 16:17], PR[:, 0:8], axis=AX.X),
           w0=("ac", ac_pr))
        t2(lambda e: e.tensor_scalar(PR[:, 8:16], PR[:, 0:8], PR[:, 16:17],
                                     None, ALU.is_ge))
        t2(lambda e: e.scalar_tensor_tensor(PR[:, 8:16], PR[:, 8:16], -1e9,
                                            PR[:, 0:8], ALU.mult, ALU.add))
        t2(lambda e: e.reduce_max(PR[:, 17:18], PR[:, 8:16], axis=AX.X))
        t2(lambda e: e.tensor_tensor(PR[:, 18:19], PR[:, 16:17], PR[:, 17:18],
                                     ALU.add))
        t2(lambda e: e.reciprocal(PR[:, 19:20], PR[:, 18:19]))
        t2(lambda e: e.tensor_scalar(PR[:, 8:16], PR[:, 0:8], PR[:, 17:18],
                                     None, ALU.is_ge))
        t2(lambda e: e.tensor_tensor(PR[:, 8:16], PR[:, 8:16], PR[:, 0:8],
                                     ALU.mult))
        t2(lambda e, tt=tt: e.tensor_scalar(WTM[:, tt * 8:tt * 8 + 8],
                                            PR[:, 8:16], PR[:, 19:20],
                                            None, ALU.mult))
        cons["top2"] = N["dv"]
        wtm_marks[tt] = N["dv"]
    h2_dv = N["dv"]
    wtm_dv = N["dv"]

    pe_attn_end = N["pe"]
    prev_gu = [pe_attn_end] * 16
    prev_wd = [pe_attn_end] * 16
    ye7_st = {}
    for ex in range(-1, NE):
        srcs = (sg_d, su_d, sd_d) if ex < 0 else (wg_d[ex], wu_d[ex], wd_d[ex])
        wl_gu = [None] * 16
        wl_wd = [None] * 16
        for j in range(16):
            wt_g = [("pe", prev_gu[j])] if (ex >= 0 or j == 0) else []
            emit("gpsimd", lambda e, s0=srcs[0], j=j: e.dma_start(
                Cc[:, j * 1024:(j + 1) * 1024], s0[:, j * 1024:(j + 1) * 1024]),
                waits=wt_g, incs=[(f"wj{j}", 16)])
            wt_u = [("dv", dv_rz)] if (ex < 0 and j == 0) else []
            emit("gpsimd", lambda e, s1=srcs[1], j=j: e.dma_start(
                Dd[:, j * 1024:(j + 1) * 1024], s1[:, j * 1024:(j + 1) * 1024]),
                waits=wt_u, incs=[(f"wj{j}", 16)])
            wl_gu[j] = (f"wj{j}", N[f"wj{j}"])
            wt_d = [("pe", prev_wd[j])] if (ex >= 0 or j == 0) else []
            emit("gpsimd", lambda e, s2=srcs[2], j=j: e.dma_start(
                Bv[:, j * 1024:(j + 1) * 1024], s2[:, j * 1024:(j + 1) * 1024]),
                waits=wt_d, incs=[(f"wd{j}", 16)])
            wl_wd[j] = (f"wd{j}", N[f"wd{j}"])
        if ex < 0:
            emit("gpsimd", lambda e: e.dma_start(BRW[:, :], brt_d), incs=[("wb", 16)])
        new_gu = [0] * 16
        new_wd = [0] * 16
        hh_mark = [0] * 16
        for g in range(8):
            t0 = g * 512
            for ich in range(16):
                ga = ich % 2
                ps_g = ga * 1024
                wt = [("dv", h2_dv),
                      ("dv", cons.get(("hhw", ga), 0))]
                if g == 0:
                    wt.append(wl_gu[ich])
                for cc in range(8):
                    emit("tensor", lambda e, ich=ich, cc=cc, t0=t0, ps_g=ps_g: e.matmul(
                        psA[:, ps_g:ps_g + 512],
                        Cc[:, ich * 1024 + cc * 128: ich * 1024 + (cc + 1) * 128],
                        A[:, cc * 4096 + t0: cc * 4096 + t0 + 512],
                        start=(cc == 0), stop=(cc == 7)),
                        waits=(wt if cc == 0 else []))
                for cc in range(8):
                    emit("tensor", lambda e, ich=ich, cc=cc, t0=t0, ps_g=ps_g: e.matmul(
                        psA[:, ps_g + 512:ps_g + 1024],
                        Dd[:, ich * 1024 + cc * 128: ich * 1024 + (cc + 1) * 128],
                        A[:, cc * 4096 + t0: cc * 4096 + t0 + 512],
                        start=(cc == 0), stop=(cc == 7)),
                        incs=([("pe", 1)] if cc == 7 else []))
                pe_gu = N["pe"]
                if g == 7:
                    new_gu[ich] = pe_gu
                emit("scalar", lambda e, ps_g=ps_g: e.activation(
                    TMP[:, 0:512], psA[:, ps_g:ps_g + 512], AF.Silu),
                    waits=[("pe", pe_gu), ("dv", cons.get("silu", 0)),
                           ("ac", N["ac"])],
                    incs=[("ac", 1)])
                ac_si = N["ac"]
                emit("vector", lambda e, ich=ich, ps_g=ps_g: e.tensor_tensor(
                    HH[:, ich * 512:(ich + 1) * 512], TMP[:, 0:512],
                    psA[:, ps_g + 512:ps_g + 1024], ALU.mult),
                    waits=[("ac", ac_si), ("dv", N["dv"])],
                    incs=[("dv", 1)])
                cons["silu"] = N["dv"]
                cons[("hhw", ga)] = N["dv"]
                hh_mark[ich] = N["dv"]
            for sub in range(4):
                slot = sub % 2
                tt = g * 4 + sub
                for ich in range(16):
                    for hf in range(2):
                        wthis = []
                        if ich == 0 and hf == 0:
                            wthis.append(("dv", cons.get(("psBy", slot), 0)))
                        if sub == 0 and hf == 0:
                            wthis.append(("dv", hh_mark[ich]))
                            if g == 0:
                                wthis.append(wl_wd[ich])
                        do_inc = (ich == 15 and hf == 1) or (g == 7 and sub == 3 and hf == 1)
                        emit("tensor", lambda e, ich=ich, hf=hf, sub=sub, slot=slot: e.matmul(
                            psB[:, slot * 1024 + hf * 512: slot * 1024 + hf * 512 + 512],
                            HH[:, ich * 512 + sub * 128: ich * 512 + sub * 128 + 128],
                            Bv[:, ich * 1024 + hf * 512: ich * 1024 + hf * 512 + 512],
                            start=(ich == 0), stop=(ich == 15)),
                            waits=wthis,
                            incs=([("pe", 1)] if do_inc else []))
                        if g == 7 and sub == 3 and hf == 1:
                            new_wd[ich] = N["pe"]
                pe_y = N["pe"]
                ysem = "ya" if slot == 0 else "yb"
                if ex < 0:
                    emit("vector", lambda e, slot=slot: e.tensor_copy(
                        YS[:, slot * 1024:(slot + 1) * 1024],
                        psB[:, slot * 1024: slot * 1024 + 1024]),
                        waits=[("pe", pe_y), (ysem, N[ysem]),
                               ("dv", N["dv"]),
                               ("x1a" if slot == 0 else "x1b",
                                N["x1a" if slot == 0 else "x1b"])],
                        incs=[("dv", 1)])
                else:
                    emit("vector", lambda e, slot=slot, tt=tt, ex=ex: e.tensor_scalar(
                        YS[:, slot * 1024:(slot + 1) * 1024],
                        psB[:, slot * 1024: slot * 1024 + 1024],
                        WTM[:, tt * 8 + ex: tt * 8 + ex + 1], None, ALU.mult),
                        waits=[("pe", pe_y), (ysem, N[ysem]),
                               ("dv", N["dv"])],
                        incs=[("dv", 1)])
                cons[("psBy", slot)] = N["dv"]
                dv_y = N["dv"]
                dst = ysh_d if ex < 0 else ye_d
                roff = (0 if ex < 0 else ex * TC) + tt * 128
                emit("sync", lambda e, dst=dst, roff=roff, slot=slot: e.dma_start(
                    dst[roff:roff + 128, :], YS[:, slot * 1024:(slot + 1) * 1024]),
                    waits=[("dv", dv_y)], incs=[(ysem, 16)])
                if ex == NE - 1:
                    ye7_st[tt] = (N["ya"], N["yb"])
        prev_gu = new_gu
        prev_wd = new_wd
    ffn_st = (N["ya"], N["yb"])

    for tt in range(NT):
        tok = tt * 128
        srcs = [(x1_d, tok), (ysh_d, tok)] + [(ye_d, e * TC + tok) for e in range(NE)]
        for i, (srcd, roff) in enumerate(srcs):
            if i < 8:
                dstap = HH[:, i * 1024:(i + 1) * 1024]
            else:
                dstap = YS[:, (i - 8) * 1024:(i - 7) * 1024]
            emit("sync", lambda e, srcd=srcd, roff=roff, dstap=dstap: e.dma_start(
                dstap, srcd[roff:roff + 128, :]),
                waits=[("ya", ffn_st[0]), ("yb", ffn_st[1]),
                       ("x1a", x1_st[0]), ("x1b", x1_st[1]),
                       ("dv", cons.get("fin", 0)),
                       ("po", cons.get("finp", 0))],
                incs=[("fl", 16)])
        ld_f = N["fl"]
        emit("vector", lambda e: e.tensor_tensor(
            HH[:, 0:1024], HH[:, 0:1024], HH[:, 1024:2048], ALU.add),
            waits=[("fl", ld_f), ("dv", N["dv"])], incs=[("dv", 1)])
        emit("gpsimd", lambda e: e.tensor_tensor(
            HH[:, 2048:3072], HH[:, 2048:3072], HH[:, 3072:4096], ALU.add),
            waits=[("fl", ld_f), ("po", N["po"])], incs=[("po", 1)])
        emit("vector", lambda e: e.tensor_tensor(
            HH[:, 4096:5120], HH[:, 4096:5120], HH[:, 5120:6144], ALU.add),
            waits=[("dv", N["dv"])], incs=[("dv", 1)])
        emit("gpsimd", lambda e: e.tensor_tensor(
            HH[:, 6144:7168], HH[:, 6144:7168], HH[:, 7168:8192], ALU.add),
            waits=[("po", N["po"])], incs=[("po", 1)])
        emit("vector", lambda e: e.tensor_tensor(
            YS[:, 0:1024], YS[:, 0:1024], YS[:, 1024:2048], ALU.add),
            waits=[("dv", N["dv"])], incs=[("dv", 1)])
        l1dv = N["dv"]
        l1po = N["po"]
        emit("vector", lambda e: e.tensor_tensor(
            HH[:, 1024:2048], HH[:, 0:1024], HH[:, 2048:3072], ALU.add),
            waits=[("dv", l1dv), ("po", l1po)], incs=[("dv", 1)])
        emit("vector", lambda e: e.tensor_tensor(
            HH[:, 3072:4096], HH[:, 4096:5120], HH[:, 6144:7168], ALU.add),
            waits=[("dv", N["dv"])], incs=[("dv", 1)])
        emit("vector", lambda e: e.tensor_tensor(
            HH[:, 5120:6144], HH[:, 1024:2048], HH[:, 3072:4096], ALU.add),
            waits=[("dv", N["dv"])], incs=[("dv", 1)])
        emit("vector", lambda e: e.tensor_tensor(
            G[:, :], HH[:, 5120:6144], YS[:, 0:1024], ALU.add),
            waits=[("dv", N["dv"]), ("out", cons.get("outG", 0))],
            incs=[("dv", 1)])
        cons["fin"] = N["dv"]
        cons["finp"] = N["po"]
        dv_f = N["dv"]
        emit("sync", lambda e, tok=tok: e.dma_start(
            out_d[tok:tok + 128, :], G[:, :]),
            waits=[("dv", dv_f)], incs=[("out", 16)])
        cons["outG"] = N["out"]

    wait_only("sync", [("out", N["out"]), ("x1a", N["x1a"]), ("x1b", N["x1b"]),
                       ("ya", N["ya"]), ("yb", N["yb"])])

    with ctx:
        ctx.enter_context(nc.allow_low_precision(
            "bf16 softmax-denominator reciprocal; within checked tolerance"))
        block = ctx.enter_context(nc.Block())

        @block.sync
        def _(e):
            for fn in OPS["sync"]:
                fn(e)

        @block.gpsimd
        def _(e):
            for fn in OPS["gpsimd"]:
                fn(e)

        @block.tensor
        def _(e):
            for fn in OPS["tensor"]:
                fn(e)

        @block.scalar
        def _(e):
            for fn in OPS["scalar"]:
                fn(e)

        @block.vector
        def _(e):
            for fn in OPS["vector"]:
                fn(e)

    return nc


def kernel(**inputs):
    w = prep_weights(inputs)
    x = np.ascontiguousarray(inputs["L0_attn_x"]).reshape(NCORES, TC, CM)
    nc = build_program()
    base = {k: w[k] for k in ("wk", "wv", "wq", "wout", "bqk", "bv", "bout",
                              "brt", "gbq", "gbk", "wrt", "sg", "su", "sd",
                              "wg", "wu", "wd")}
    in_maps = [dict(base, x=np.ascontiguousarray(x[c])) for c in range(NCORES)]
    res = run_bass_kernel_spmd(nc, in_maps, core_ids=list(range(NCORES)))
    out = np.stack([res.results[c]["out"] for c in range(NCORES)])
    return out.reshape(16, L, CM)


# version 12
# speedup vs baseline: 1.1967x; 1.1967x over previous
import numpy as np
import ml_dtypes
from contextlib import ExitStack

import concourse.bass as bass
import concourse.mybir as mybir
from concourse.bass_utils import run_bass_kernel_spmd

BFnp = ml_dtypes.bfloat16
F32 = mybir.dt.float32
BF16 = mybir.dt.bfloat16
AF = mybir.ActivationFunctionType
ALU = mybir.AluOpType
AX = mybir.AxisListType

NCORES = 8
BC = 2
L = 2048
CM = 1024
NE = 8
TC = BC * L
NT = TC // 128
NTS = L // 128
LN0125 = float(np.log(0.125))

DEBUG_X1 = False
DEBUG_ATT = False


def _perm_qk():
    perm = np.zeros(CM, dtype=np.int64)
    for cc in range(8):
        base = cc * 128
        for half in range(2):
            h = 2 * cc + half
            for j in range(32):
                perm[base + half * 64 + j] = h * 64 + 2 * j
                perm[base + half * 64 + 32 + j] = h * 64 + 2 * j + 1
    return perm


def _dperm(p):
    return 2 * (p % 32) + ((p % 64) // 32)


def prep_weights(inp):
    w = {}
    perm = _perm_qk()
    w_qkv = inp["L0_attn_w_qkv"]
    b_qkv = inp["L0_attn_b_qkv"]
    wqp = w_qkv[0:1024][perm]
    wkp = w_qkv[1024:2048][perm]
    wvp = w_qkv[2048:3072]
    w["wk"] = np.ascontiguousarray(
        wkp.reshape(8, 128, 8, 128).transpose(3, 0, 2, 1).reshape(128, 8192))
    w["wv"] = np.ascontiguousarray(
        wvp.T.reshape(8, 128, 1024).transpose(1, 0, 2).reshape(128, 8192))
    w["wq"] = np.ascontiguousarray(
        wqp.reshape(8, 128, 8, 128).transpose(0, 3, 2, 1).reshape(8, 128, 1024))
    w["wout"] = np.ascontiguousarray(
        inp["L0_attn_w_out"].T.reshape(8, 128, 1024).transpose(1, 0, 2).reshape(128, 8192))
    w["bqk"] = np.ascontiguousarray(
        np.stack([b_qkv[0:1024][perm].reshape(8, 128).T,
                  b_qkv[1024:2048][perm].reshape(8, 128).T], axis=-1).reshape(128, 16))
    w["bv"] = inp["L0_attn_b_qkv"][2048:3072].reshape(1, 1024)
    w["bout"] = inp["L0_attn_b_out"].reshape(1, 1024)
    w["brt"] = np.ascontiguousarray(
        np.pad(inp["L0_ffn_routing_bias"], (0, 1016)).reshape(1, 1024))
    def gmat(g):
        m = np.zeros((16, 1024), dtype=np.float32)
        for cc in range(8):
            for p in range(128):
                h = 2 * cc + p // 64
                m[h, cc * 128 + p] = g[h, _dperm(p)]
        return m
    w["gbq"] = gmat(inp["L0_attn_q_gamma"])
    w["gbk"] = gmat(inp["L0_attn_k_gamma"])
    w["wrt"] = np.ascontiguousarray(
        inp["L0_ffn_router_w"].T.reshape(8, 128, 8).transpose(1, 0, 2).reshape(128, 64))

    def wgl(m):
        return np.ascontiguousarray(
            m.reshape(16, 128, 8, 128).transpose(3, 0, 2, 1).reshape(128, 16384))

    def wdl(m):
        return np.ascontiguousarray(
            m.T.reshape(16, 128, 1024).transpose(1, 0, 2).reshape(128, 16384))
    w["sg"] = wgl(inp["L0_ffn_sg_w"])
    w["su"] = wgl(inp["L0_ffn_su_w"])
    w["sd"] = wdl(inp["L0_ffn_sd_w"])
    w["wg"] = np.stack([wgl(inp["L0_ffn_wg"][e]) for e in range(NE)])
    w["wu"] = np.stack([wgl(inp["L0_ffn_wu"][e]) for e in range(NE)])
    w["wd"] = np.stack([wdl(inp["L0_ffn_wd"][e]) for e in range(NE)])
    return w


def _consts():
    c = {}
    inv = 1.0 / (10000.0 ** (np.arange(32, dtype=np.float64) / 32))
    ang = np.arange(L, dtype=np.float64)[:, None] * inv[None, :]
    cos = np.cos(ang).T
    sin = np.sin(ang).T
    c["cs1"] = np.concatenate([cos, cos, cos, cos], 0).astype(BFnp)
    c["cs2"] = np.concatenate([-sin, sin, -sin, sin], 0).astype(BFnp)
    c["idf"] = np.eye(128, dtype=np.float32)
    c["on1"] = np.ones((1, 128), dtype=BFnp)
    blk = np.zeros((128, 128), dtype=BFnp)
    for cc in range(8):
        for p in range(128):
            blk[p, cc * 16 + 2 * cc + p // 64] = 1
    c["blk"] = blk
    hs = np.zeros((16, 1024), dtype=BFnp)
    for oc in range(8):
        for p in range(128):
            hs[2 * oc + p // 64, oc * 128 + p] = 1
    c["hs"] = hs
    return c


def build_program():
    nc = bass.Bass("TRN2", target_bir_lowering=False)

    x_d = nc.dram_tensor("x", [TC, CM], F32, kind="ExternalInput").ap()
    out_d = nc.dram_tensor("out", [TC, CM], F32, kind="ExternalOutput").ap()
    wk_d = nc.dram_tensor("wk", [128, 8192], F32, kind="ExternalInput").ap()
    wv_d = nc.dram_tensor("wv", [128, 8192], F32, kind="ExternalInput").ap()
    wq_d = nc.dram_tensor("wq", [8, 128, 1024], F32, kind="ExternalInput").ap()
    wout_d = nc.dram_tensor("wout", [128, 8192], F32, kind="ExternalInput").ap()
    bqk_d = nc.dram_tensor("bqk", [128, 16], F32, kind="ExternalInput").ap()
    bv_d = nc.dram_tensor("bv", [1, 1024], F32, kind="ExternalInput").ap()
    bout_d = nc.dram_tensor("bout", [1, 1024], F32, kind="ExternalInput").ap()
    brt_d = nc.dram_tensor("brt", [1, 1024], F32, kind="ExternalInput").ap()
    gbq_d = nc.dram_tensor("gbq", [16, 1024], F32, kind="ExternalInput").ap()
    gbk_d = nc.dram_tensor("gbk", [16, 1024], F32, kind="ExternalInput").ap()
    wrt_d = nc.dram_tensor("wrt", [128, 64], F32, kind="ExternalInput").ap()
    sg_d = nc.dram_tensor("sg", [128, 16384], F32, kind="ExternalInput").ap()
    su_d = nc.dram_tensor("su", [128, 16384], F32, kind="ExternalInput").ap()
    sd_d = nc.dram_tensor("sd", [128, 16384], F32, kind="ExternalInput").ap()
    wg_d = nc.dram_tensor("wg", [NE, 128, 16384], F32, kind="ExternalInput").ap()
    wu_d = nc.dram_tensor("wu", [NE, 128, 16384], F32, kind="ExternalInput").ap()
    wd_d = nc.dram_tensor("wd", [NE, 128, 16384], F32, kind="ExternalInput").ap()

    x1_d = nc.dram_tensor("x1sp", [TC, CM], BF16,
                          kind="ExternalOutput" if DEBUG_X1 else "Internal").ap()
    ysh_d = nc.dram_tensor("ysh", [TC, CM], BF16,
                           kind="ExternalOutput" if DEBUG_ATT else "Internal").ap()
    if DEBUG_ATT:
        dbgz_d = nc.dram_tensor("dbgz", [16, 4096], BF16, kind="ExternalOutput").ap()
        dbgo_d = nc.dram_tensor("dbgo", [128, 16384], BF16, kind="ExternalOutput").ap()
        dbgq_d = nc.dram_tensor("dbgq", [128, 4096], BF16, kind="ExternalOutput").ap()
        dbgk_d = nc.dram_tensor("dbgk", [128, 16384], BF16, kind="ExternalOutput").ap()
    ye_d = nc.dram_tensor("ye", [NE * TC, CM], BF16,
                          kind="ExternalOutput" if DEBUG_ATT else "Internal").ap()

    ct = _consts()
    cs1_d = nc.inline_tensor(ct["cs1"], "cs1c").ap()
    cs2_d = nc.inline_tensor(ct["cs2"], "cs2c").ap()
    idf_d = nc.inline_tensor(ct["idf"], "idfc").ap()
    on1_d = nc.inline_tensor(ct["on1"], "on1c").ap()
    blk_d = nc.inline_tensor(ct["blk"], "blkc").ap()
    hs_d = nc.inline_tensor(ct["hs"], "hsc").ap()

    ctx = ExitStack()
    sb = lambda n, sh, dt: ctx.enter_context(nc.sbuf_tensor(n, sh, dt))
    A = sb("Ar", [128, 32768], BF16)
    Bv = sb("Bv", [128, 16640], BF16)
    Cc = sb("Cc", [128, 16384], BF16)
    Dd = sb("Dd", [128, 16384], BF16)
    HH = sb("HH", [128, 8192], BF16)
    YS = sb("YS", [128, 2048], BF16)
    XT = sb("XT", [128, 2048], BF16)
    G = sb("G", [128, 1024], F32)
    TMP = sb("TMP", [128, 1536], BF16)
    CS1 = sb("CS1", [128, 2048], BF16)
    CS2 = sb("CS2", [128, 2048], BF16)
    IDF = sb("IDF", [128, 128], F32)
    ON1 = sb("ON1", [1, 128], BF16)
    BLK = sb("BLK", [128, 128], BF16)
    GB = sb("GB", [16, 1024], BF16)
    BQK = sb("BQK", [128, 16], F32)
    BRW = sb("BRW", [1, 1024], BF16)
    WRT = sb("WRT", [128, 64], BF16)
    RST = sb("RST", [16, 512], BF16)
    STT = sb("STT", [128, 8], F32)
    PR = sb("PR", [128, 24], F32)
    CB = sb("CB", [128, 4], F32)
    WTM = sb("WTM", [128, 256], F32)

    psA = ctx.enter_context(nc.psum_tensor("psA", [128, 2048], F32))
    psB = ctx.enter_context(nc.psum_tensor("psB", [128, 2048], F32))

    sems = {}
    semnames = (["ld", "pe", "ac", "dv", "po", "wb", "xa", "xb", "qa", "qb",
                 "sw", "z", "ob", "x1a", "x1b", "pa", "pb", "ya", "yb",
                 "fl", "out"]
                + [f"wj{j}" for j in range(16)] + [f"wd{j}" for j in range(16)])
    for s in semnames:
        sems[s] = ctx.enter_context(nc.semaphore("s_" + s))
    N = {k: 0 for k in sems}
    OPS = {e: [] for e in ("tensor", "vector", "scalar", "gpsimd", "sync")}

    def emit(eng, build, waits=(), incs=()):
        w = tuple(waits)
        i = tuple(incs)

        def fn(e, _b=build, _w=w, _i=i):
            for s, v in _w:
                if v > 0:
                    e.wait_ge(sems[s], v)
            inst = _b(e)
            for s, a in _i:
                inst.then_inc(sems[s], a)
        OPS[eng].append(fn)
        for s, a in i:
            N[s] += a

    def wait_only(eng, waits):
        w = tuple(waits)

        def fn(e, _w=w):
            for s, v in _w:
                if v > 0:
                    e.wait_ge(sems[s], v)
        OPS[eng].append(fn)

    cons = {}

    for dst, src in ((CS1, cs1_d), (CS2, cs2_d), (IDF, idf_d), (ON1, on1_d),
                     (BLK, blk_d), (BQK, bqk_d)):
        emit("sync", lambda e, dst=dst, src=src: e.dma_start(dst[:, :], src),
             incs=[("ld", 16)])
    p0_ld = N["ld"]
    emit("gpsimd", lambda e: e.dma_start(WRT[:, :], wrt_d), incs=[("wb", 16)])
    p0_wrt = N["wb"]
    emit("vector",
         lambda e: e.memset(Bv[:, :].rearrange(
             "p (m c) -> p m c", c=65)[:, :, 64:65], 1.0),
         incs=[("dv", 1)])
    for col, val in ((0, 1e-5), (1, 1e-6), (2, -8.0), (3, LN0125)):
        emit("vector", lambda e, col=col, val=val: e.memset(
            CB[:, col:col + 1], val), incs=[("dv", 1)])
    vones_dv = N["dv"]

    def ln_tile(src_ap, pe_war_key, extra_waits=()):
        emit("scalar", lambda e: e.activation(
            G[:, :], src_ap, AF.Copy, accum_out=STT[:, 0:1]),
            waits=list(extra_waits) + [("ac", N["ac"]),
                                       ("pe", cons.get(pe_war_key, 0)),
                                       ("dv", cons.get("lnstt", 0))],
            incs=[("ac", 1)])
        emit("scalar", lambda e: e.activation(
            G[:, :], src_ap, AF.Square, accum_out=STT[:, 1:2]),
            waits=[("ac", N["ac"])], incs=[("ac", 1)])
        ac_st = N["ac"]
        emit("vector", lambda e: e.tensor_scalar(
            STT[:, 2:3], STT[:, 0:1], 1.0 / 1024, None, ALU.mult),
            waits=[("ac", ac_st), ("dv", N["dv"])], incs=[("dv", 1)])
        emit("vector", lambda e: e.tensor_tensor(
            STT[:, 3:4], STT[:, 2:3], STT[:, 2:3], ALU.mult),
            waits=[("dv", N["dv"])], incs=[("dv", 1)])
        emit("vector", lambda e: e.scalar_tensor_tensor(
            STT[:, 4:5], STT[:, 1:2], 1.0 / 1024, STT[:, 3:4],
            ALU.mult, ALU.subtract),
            waits=[("dv", N["dv"])], incs=[("dv", 1)])
        dv_var = N["dv"]
        emit("scalar", lambda e: e.activation(
            STT[:, 5:6], STT[:, 4:5], AF.Ln, bias=CB[:, 0:1]),
            waits=[("dv", dv_var), ("ac", N["ac"])], incs=[("ac", 1)])
        emit("scalar", lambda e: e.activation(
            STT[:, 6:7], STT[:, 5:6], AF.Exp, scale=-0.5),
            waits=[("ac", N["ac"])], incs=[("ac", 1)])
        ac_r = N["ac"]
        emit("vector", lambda e: e.tensor_scalar(
            G[:, :], src_ap, STT[:, 2:3], STT[:, 6:7], ALU.subtract, ALU.mult),
            waits=[("ac", ac_r), ("dv", N["dv"])], incs=[("dv", 1)])
        cons["lnstt"] = N["dv"]
        return N["dv"]

    def transpose_to(dst_base_fn, dv_h):
        for cc in range(8):
            wt = [("dv", dv_h)]
            if cc == 0:
                wt.append(("dv", cons.get("psAtr", 0)))
                wt.append(("ld", p0_ld))
            emit("tensor", lambda e, cc=cc: e.transpose(
                psA[:, cc * 128:(cc + 1) * 128],
                G[:, cc * 128:(cc + 1) * 128], IDF[:, :]),
                waits=(wt if cc == 0 else []),
                incs=([("pe", 1)] if cc == 7 else []))
        cons["htr"] = N["pe"]
        pe_tr = N["pe"]
        for cc in range(8):
            emit("vector", lambda e, cc=cc: e.tensor_copy(
                dst_base_fn(cc), psA[:, cc * 128:(cc + 1) * 128]),
                waits=([("pe", pe_tr), ("dv", N["dv"])] if cc == 0 else []),
                incs=([("dv", 1)] if cc == 7 else []))
        cons["psAtr"] = N["dv"]
        return N["dv"]

    def rope_rms(kind, psrc, ps_free_key, bias_col, cs_col, out_col, out_buf,
                 blk_col, sum_start, sum_stop, exp_bias):
        emit("scalar", lambda e, psrc=psrc: e.activation(
            TMP[:, 0:512], psrc, AF.Identity, bias=BQK[:, bias_col:bias_col + 1]),
            waits=[("dv", cons.get("tmprope", 0)), ("ac", N["ac"]),
                   ("pe", cons.get("sqmm", 0)), ("ld", p0_ld),
                   ("po", cons.get("swfree", 0))],
            incs=[("ac", 1)])
        cons[ps_free_key] = N["ac"]
        ac_raw = N["ac"]
        for qr in range(4):
            dq = qr ^ 1
            emit("sync", lambda e, qr=qr, dq=dq: e.dma_start(
                TMP[dq * 32:(dq + 1) * 32, 512:1024],
                TMP[qr * 32:(qr + 1) * 32, 0:512]),
                waits=([("ac", ac_raw), ("po", cons.get("swfree", 0))]
                       if qr == 0 else []),
                incs=[("sw", 16)])
        ld_sw = N["sw"]
        emit("vector", lambda e, cs_col=cs_col: e.tensor_tensor(
            TMP[:, 1024:1536], TMP[:, 0:512], CS1[:, cs_col:cs_col + 512], ALU.mult),
            waits=[("ac", ac_raw), ("dv", N["dv"])], incs=[("dv", 1)])
        dv_t1 = N["dv"]
        emit("gpsimd", lambda e, cs_col=cs_col: e.tensor_tensor(
            TMP[:, 0:512], TMP[:, 512:1024], CS2[:, cs_col:cs_col + 512], ALU.mult),
            waits=[("sw", ld_sw), ("dv", dv_t1), ("po", N["po"])],
            incs=[("po", 1)])
        cons["swfree"] = N["po"]
        po_t2 = N["po"]
        oc = out_col
        emit("vector", lambda e, oc=oc, ob=out_buf: e.tensor_tensor(
            ob[:, oc:oc + 512], TMP[:, 1024:1536], TMP[:, 0:512], ALU.add),
            waits=[("po", po_t2), ("dv", N["dv"])], incs=[("dv", 1)])
        cons["tmprope"] = N["dv"]
        po_r = N["dv"]
        emit("scalar", lambda e, oc=oc, ob=out_buf: e.activation(
            TMP[:, 0:512], ob[:, oc:oc + 512], AF.Square),
            waits=[("dv", po_r), ("ac", N["ac"]), ("pe", cons.get("sqmm", 0))],
            incs=[("ac", 1)])
        ac_sq = N["ac"]
        wt = [("ac", ac_sq)]
        if sum_start:
            wt += [("dv", cons.get("ssfree_dv", 0)), ("ac", cons.get("ssfree_ac", 0)),
                   ("dv", cons.get(("psBv", 0), 0)), ("dv", cons.get("psBo", 0))]
        emit("tensor", lambda e, blk_col=blk_col: e.matmul(
            psB[0:16, 0:512], BLK[:, blk_col:blk_col + 16], TMP[:, 0:512],
            start=sum_start, stop=sum_stop),
            waits=wt, incs=([("pe", 1)] if sum_stop else []))
        cons["sqmm"] = N["pe"]
        if not sum_stop:
            return None
        pe_ss = N["pe"]
        emit("scalar", lambda e: e.activation(
            psB[0:16, 512:1024], psB[0:16, 0:512], AF.Ln, bias=CB[0:16, 1:2],
            scale=1.0 / 64),
            waits=[("pe", pe_ss), ("ac", N["ac"])], incs=[("ac", 1)])
        if exp_bias == 0.0:
            emit("scalar", lambda e: e.activation(
                RST[0:16, 0:512], psB[0:16, 512:1024], AF.Exp, scale=-0.5),
                waits=[("ac", N["ac"]), ("pe", cons.get("rstfree", 0))],
                incs=[("ac", 1)])
        else:
            emit("scalar", lambda e: e.activation(
                RST[0:16, 0:512], psB[0:16, 512:1024], AF.Exp, scale=-0.5,
                bias=CB[0:16, 3:4]),
                waits=[("ac", N["ac"])], incs=[("ac", 1)])
        cons["ssfree_ac"] = N["ac"]
        return N["ac"]

    def rms_apply(ac_rstd, gb_off, out_col, out_buf):
        emit("tensor", lambda e, gb_off=gb_off: e.matmul(
            psB[:, 1024:1536], GB[0:16, gb_off:gb_off + 128], RST[0:16, 0:512],
            start=True, stop=True),
            waits=[("ac", ac_rstd), ("dv", cons.get("bcfree", 0)),
                   ("dv", cons.get(("psBv", 1), 0))],
            incs=[("pe", 1)])
        cons["rstfree"] = N["pe"]
        pe_bc = N["pe"]
        emit("vector", lambda e, oc=out_col, ob=out_buf: e.tensor_tensor(
            ob[:, oc:oc + 512], ob[:, oc:oc + 512], psB[:, 1024:1536], ALU.mult),
            waits=[("pe", pe_bc), ("dv", N["dv"])], incs=[("dv", 1)])
        cons["bcfree"] = N["dv"]
        cons["ssfree_dv"] = N["dv"]
        return N["dv"]

    for s in range(BC):
        pre_wt = [("pe", N["pe"]), ("dv", N["dv"])]
        emit("gpsimd", lambda e: e.dma_start(Cc[:, 0:8192], wk_d),
             waits=pre_wt, incs=[("wb", 16)])
        emit("gpsimd", lambda e: e.dma_start(Cc[:, 8192:16384], wv_d),
             incs=[("wb", 16)])
        emit("gpsimd", lambda e: e.dma_start(BRW[:, :], bv_d), incs=[("wb", 16)])
        emit("gpsimd", lambda e: e.dma_start(GB[:, :], gbk_d), incs=[("wb", 16)])
        wl_kv = N["wb"]
        emit("vector",
             lambda e: e.memset(Bv[:, :].rearrange(
                 "p (m c) -> p m c", c=65)[:, :, 64:65], 1.0),
             waits=[("pe", N["pe"]), ("dv", N["dv"])], incs=[("dv", 1)])
        vones_dv = N["dv"]

        h_dv = {}
        for g in range(4):
            for ti in range(4):
                tt = g * 4 + ti
                tok = s * L + tt * 128
                gs = tt % 2
                xsem = "xa" if gs == 0 else "xb"
                emit("gpsimd", lambda e, tok=tok, gs=gs: e.dma_start(
                    XT[:, gs * 1024:(gs + 1) * 1024], x_d[tok:tok + 128, :]),
                    waits=[("dv", cons.get(("XT", gs), 0)),
                           ("pe", N["pe"] if (s == 1 and tt == 0) else 0)],
                    incs=[(xsem, 16)])
                dv_h = ln_tile(XT[:, gs * 1024:(gs + 1) * 1024], "htr",
                               extra_waits=[(xsem, N[xsem])])
                cons[("XT", gs)] = dv_h
                transpose_to(
                    (lambda cc, tt=tt: A[:, cc * 2048 + tt * 128:
                                         cc * 2048 + tt * 128 + 128]), dv_h)
                h_dv[tt] = N["dv"]
            h_rdy = h_dv[g * 4 + 3]
            for sub in range(4):
                tt = g * 4 + sub
                slot = sub % 2
                wt = [("dv", h_rdy), ("wb", wl_kv), ("ld", p0_ld),
                      ("dv", cons.get(("psBv", slot), 0)),
                      ("ac", cons.get("ssfree_ac", 0)),
                      ("dv", cons.get("bcfree", 0)),
                      ("dv", vones_dv),
                      ("dv", cons.get("psBo", 0)),
                      ("dv", cons.get("bk6", 0)),
                      ("ac", cons.get("psq", 0))]
                first = True
                for cc in range(8):
                    for hf in range(2):
                        emit("tensor", lambda e, cc=cc, hf=hf, slot=slot, tt=tt: e.matmul(
                            psB[:, slot * 1024 + hf * 512: slot * 1024 + hf * 512 + 512],
                            A[:, cc * 2048 + tt * 128: cc * 2048 + tt * 128 + 128],
                            Cc[:, 8192 + cc * 1024 + hf * 512:
                               8192 + cc * 1024 + hf * 512 + 512],
                            start=(cc == 0), stop=False),
                            waits=(wt if first else []))
                        first = False
                for hf in range(2):
                    emit("tensor", lambda e, hf=hf, slot=slot: e.matmul(
                        psB[:, slot * 1024 + hf * 512: slot * 1024 + hf * 512 + 512],
                        ON1[0:1, :], BRW[0:1, hf * 512:(hf + 1) * 512],
                        start=False, stop=True),
                        incs=([("pe", 1)] if hf == 1 else []))
                pe_v = N["pe"]
                emit("vector", lambda e, slot=slot, tt=tt: e.tensor_copy(
                    Bv[:, tt * 1040:(tt + 1) * 1040].rearrange(
                        "p (h c) -> p h c", c=65)[:, :, 0:64],
                    psB[:, slot * 1024: slot * 1024 + 1024].rearrange(
                        "p (h c) -> p h c", c=64)),
                    waits=[("pe", pe_v), ("dv", N["dv"])], incs=[("dv", 1)])
                cons[("psBv", slot)] = N["dv"]
            lc = g * 512
            for kc in range(8):
                slot = kc % 2
                ps0 = 1024 + slot * 512
                wt = [("dv", h_rdy), ("ac", cons.get(("psAk", slot), 0)),
                      ("wb", wl_kv)]
                for cc in range(8):
                    emit("tensor", lambda e, cc=cc, kc=kc, ps0=ps0, lc=lc: e.matmul(
                        psA[:, ps0:ps0 + 512],
                        Cc[:, kc * 1024 + cc * 128: kc * 1024 + (cc + 1) * 128],
                        A[:, cc * 2048 + lc: cc * 2048 + lc + 512],
                        start=(cc == 0), stop=(cc == 7)),
                        waits=(wt if cc == 0 else []),
                        incs=([("pe", 1)] if cc == 7 else []))
                pe_k = N["pe"]
                wait_only("scalar", [("pe", pe_k)])
                ac_rstd = rope_rms(
                    "k", psA[:, ps0:ps0 + 512], ("psAk", slot),
                    bias_col=2 * kc + 1, cs_col=lc,
                    out_col=16384 + kc * 2048 + lc, out_buf=A,
                    blk_col=kc * 16, sum_start=(kc == 0), sum_stop=(kc == 7),
                    exp_bias=0.0)
            for kc in range(8):
                rms_apply(ac_rstd, kc * 128, 16384 + kc * 2048 + lc, A)
        k_done = N["dv"]

        emit("gpsimd", lambda e: e.dma_start(GB[:, :], gbq_d),
             waits=[("pe", N["pe"])], incs=[("wb", 16)])
        wl_gbq = N["wb"]

        for hp in range(8):
            wqs = hp % 2
            qsem = "qa" if wqs == 0 else "qb"
            emit("gpsimd", lambda e, hp=hp, wqs=wqs: e.dma_start(
                Dd[:, 4096 + wqs * 1024: 4096 + (wqs + 1) * 1024], wq_d[hp]),
                waits=[("pe", cons.get(("wq", wqs), 0))], incs=[(qsem, 16)])
            wl_wq = (qsem, N[qsem])
            q_mark = {}

            def q_mm(lt, hp=hp, wqs=wqs, wl_wq=wl_wq):
                wt = [wl_wq, ("dv", k_done),
                      ("ac", cons.get("psq", 0)), ("wb", wl_gbq),
                      ("dv", cons.get(("psBv", 1), 0))]
                for cc in range(8):
                    emit("tensor", lambda e, cc=cc, lt=lt, wqs=wqs: e.matmul(
                        psB[:, 1536:2048],
                        Dd[:, 4096 + wqs * 1024 + cc * 128:
                           4096 + wqs * 1024 + (cc + 1) * 128],
                        A[:, cc * 2048 + lt * 512: cc * 2048 + lt * 512 + 512],
                        start=(cc == 0), stop=(cc == 7)),
                        waits=(wt if cc == 0 else []),
                        incs=([("pe", 1)] if cc == 7 else []))
                cons[("wq", wqs)] = N["pe"]
                pe_q = N["pe"]
                emit("scalar", lambda e, hp=hp: e.activation(
                    TMP[:, 0:512], psB[:, 1536:2048], AF.Identity,
                    bias=BQK[:, 2 * hp:2 * hp + 1]),
                    waits=[("pe", pe_q), ("dv", cons.get("tmprope", 0)),
                           ("ac", N["ac"]), ("pe", cons.get("sqmm", 0)),
                           ("po", cons.get("swfree", 0)), ("ld", p0_ld)],
                    incs=[("ac", 1)])
                cons["psq"] = N["ac"]
                ac_raw = N["ac"]
                for qr in range(4):
                    dq = qr ^ 1
                    emit("sync", lambda e, qr=qr, dq=dq: e.dma_start(
                        TMP[dq * 32:(dq + 1) * 32, 512:1024],
                        TMP[qr * 32:(qr + 1) * 32, 0:512]),
                        waits=([("ac", ac_raw), ("po", cons.get("swfree", 0))]
                               if qr == 0 else []),
                        incs=[("sw", 16)])
                ld_sw = N["sw"]
                cs_col = lt * 512
                emit("vector", lambda e, cs_col=cs_col: e.tensor_tensor(
                    TMP[:, 1024:1536], TMP[:, 0:512], CS1[:, cs_col:cs_col + 512],
                    ALU.mult),
                    waits=[("ac", ac_raw), ("dv", N["dv"])], incs=[("dv", 1)])
                dv_t1 = N["dv"]
                emit("gpsimd", lambda e, cs_col=cs_col: e.tensor_tensor(
                    TMP[:, 0:512], TMP[:, 512:1024], CS2[:, cs_col:cs_col + 512],
                    ALU.mult),
                    waits=[("sw", ld_sw), ("dv", dv_t1), ("po", N["po"])],
                    incs=[("po", 1)])
                cons["swfree"] = N["po"]
                po_t2 = N["po"]
                qcol = lt * 512
                emit("vector", lambda e, qcol=qcol: e.tensor_tensor(
                    Dd[:, qcol:qcol + 512], TMP[:, 1024:1536], TMP[:, 0:512],
                    ALU.add),
                    waits=[("po", po_t2), ("dv", N["dv"])], incs=[("dv", 1)])
                cons["tmprope"] = N["dv"]
                q_mark[("rope", lt)] = N["dv"]

            def q_fin(lt, hp=hp):
                qcol = lt * 512
                emit("scalar", lambda e, qcol=qcol: e.activation(
                    TMP[:, 0:512], Dd[:, qcol:qcol + 512], AF.Square),
                    waits=[("dv", q_mark[("rope", lt)]), ("ac", N["ac"]),
                           ("pe", cons.get("sqmm", 0))],
                    incs=[("ac", 1)])
                q_mark[("sq", lt)] = N["ac"]
                emit("tensor", lambda e, hp=hp: e.matmul(
                    psB[0:16, 1024:1536], BLK[:, hp * 16:(hp + 1) * 16],
                    TMP[:, 0:512], start=True, stop=True),
                    waits=[("ac", q_mark[("sq", lt)]),
                           ("dv", cons.get("bk6", 0)),
                           ("ac", cons.get("bk6a", 0)),
                           ("dv", cons.get(("psBv", 1), 0)),
                           ("dv", cons.get("bcfree", 0))],
                    incs=[("pe", 1)])
                cons["sqmm"] = N["pe"]
                pe_ss = N["pe"]
                emit("scalar", lambda e: e.activation(
                    G[0:16, 0:512], psB[0:16, 1024:1536], AF.Ln, bias=CB[0:16, 1:2],
                    scale=1.0 / 64),
                    waits=[("pe", pe_ss), ("ac", N["ac"])], incs=[("ac", 1)])
                emit("scalar", lambda e: e.activation(
                    RST[0:16, 0:512], G[0:16, 0:512], AF.Exp, scale=-0.5,
                    bias=CB[0:16, 3:4]),
                    waits=[("ac", N["ac"]), ("pe", cons.get("rstfree", 0))],
                    incs=[("ac", 1)])
                ac_rstd = N["ac"]
                cons["bk6a"] = N["ac"]
                emit("tensor", lambda e, hp=hp: e.matmul(
                    psB[:, 1024:1536], GB[0:16, hp * 128:(hp + 1) * 128],
                    RST[0:16, 0:512], start=True, stop=True),
                    waits=[("ac", ac_rstd)], incs=[("pe", 1)])
                pe_bc = N["pe"]
                cons["rstfree"] = N["pe"]
                emit("vector", lambda e, qcol=lt * 512: e.tensor_tensor(
                    Dd[:, qcol:qcol + 512], Dd[:, qcol:qcol + 512],
                    psB[:, 1024:1536], ALU.mult),
                    waits=[("pe", pe_bc), ("dv", N["dv"])], incs=[("dv", 1)])
                cons["bk6"] = N["dv"]
                q_mark[lt] = N["dv"]

            def m_loop(lt, hp=hp):
                sc_mark = {}

                def emit_scores(mt):
                    half = mt % 2
                    sc0 = half * 1024
                    wt = [("dv", q_mark[lt]),
                          ("ac", cons.get(("expf", half), 0))]
                    emit("tensor", lambda e, hp=hp, mt=mt, lt=lt, sc0=sc0: e.matmul(
                        psA[:, sc0:sc0 + 512],
                        A[0:64, 16384 + hp * 2048 + mt * 128:
                          16384 + hp * 2048 + mt * 128 + 128],
                        Dd[0:64, lt * 512: lt * 512 + 512],
                        start=True, stop=True, tile_position=(0, 0)),
                        waits=wt)
                    emit("tensor", lambda e, hp=hp, mt=mt, lt=lt, sc0=sc0: e.matmul(
                        psA[:, sc0 + 512:sc0 + 1024],
                        A[64:128, 16384 + hp * 2048 + mt * 128:
                          16384 + hp * 2048 + mt * 128 + 128],
                        Dd[64:128, lt * 512: lt * 512 + 512],
                        start=True, stop=True, tile_position=(64, 0)),
                        incs=[("pe", 1)])
                    sc_mark[mt] = N["pe"]

                def emit_expav(mt):
                    half = mt % 2
                    sc0 = half * 1024
                    emit("scalar", lambda e, sc0=sc0, half=half: e.activation(
                        Dd[:, 2048 + half * 1024: 2048 + half * 1024 + 1024],
                        psA[:, sc0:sc0 + 1024], AF.Exp, bias=CB[:, 2:3]),
                        waits=[("pe", sc_mark[mt]),
                               ("pe", cons.get(("Pw", half), 0)),
                               ("ac", N["ac"])],
                        incs=[("ac", 1)])
                    cons[("expf", half)] = N["ac"]
                    ac_exp = N["ac"]
                    wt2 = [("ac", ac_exp)]
                    if mt == 0:
                        wt2.append(("dv", cons.get("psBo", 0)))
                        wt2.append(("dv", cons.get(("psBv", 0), 0)))
                        wt2.append(("ac", cons.get("ssfree_ac", 0)))
                    for hd in range(2):
                        emit("tensor", lambda e, hp=hp, mt=mt, hd=hd, half=half: e.matmul(
                            psB[0:65, hd * 512: hd * 512 + 512],
                            Bv[:, mt * 1040 + (2 * hp + hd) * 65:
                               mt * 1040 + (2 * hp + hd) * 65 + 65],
                            Dd[:, 2048 + half * 1024 + hd * 512:
                               2048 + half * 1024 + hd * 512 + 512],
                            start=(mt == 0), stop=(mt == 15)),
                            waits=(wt2 if hd == 0 else []),
                            incs=([("pe", 1)] if hd == 1 else []))
                    cons[("Pw", half)] = N["pe"]

                for mt in range(16):
                    emit_scores(mt)
                    if mt >= 1:
                        emit_expav(mt - 1)
                emit_expav(15)
                pe_av = N["pe"]
                emit("vector", lambda e: e.tensor_copy(
                    Dd[64:65, 10240:10752], psB[64:65, 0:512]),
                    waits=[("pe", pe_av), ("z", N["z"]),
                           ("dv", N["dv"])],
                    incs=[("dv", 1)])
                emit("vector", lambda e: e.tensor_copy(
                    Dd[64:65, 10752:11264], psB[64:65, 512:1024]),
                    waits=[("dv", N["dv"])], incs=[("dv", 1)])
                dv_z = N["dv"]
                emit("sync", lambda e, hp=hp, lt=lt: e.dma_start(
                    Dd[2 * hp:2 * hp + 2, 6144 + lt * 512:6144 + lt * 512 + 512],
                    Dd[64:65, 10240:11264]),
                    waits=[("dv", dv_z)], incs=[("z", 16)])
                emit("vector", lambda e, hp=hp, lt=lt: e.tensor_copy(
                    Cc[0:64, hp * 2048 + lt * 512: hp * 2048 + lt * 512 + 512],
                    psB[0:64, 0:512]),
                    waits=[("pe", pe_av), ("dv", N["dv"])], incs=[("dv", 1)])
                emit("vector", lambda e: e.tensor_copy(
                    YS[0:64, 0:512], psB[0:64, 512:1024]),
                    waits=[("pe", pe_av), ("ob", N["ob"]),
                           ("dv", N["dv"]), ("x1a", N["x1a"])],
                    incs=[("dv", 1)])
                dv_ob = N["dv"]
                emit("sync", lambda e, hp=hp, lt=lt: e.dma_start(
                    Cc[64:128, hp * 2048 + lt * 512: hp * 2048 + lt * 512 + 512],
                    YS[0:64, 0:512]),
                    waits=[("dv", dv_ob)], incs=[("ob", 16)])
                cons["psBo"] = N["dv"]

            q_mm(0)
            q_fin(0)
            q_mm(1)
            for lt in range(4):
                m_loop(lt)
                if lt + 1 < 4:
                    q_fin(lt + 1)
                if lt + 2 < 4:
                    q_mm(lt + 2)
        z_ld = N["z"]

        emit("vector", lambda e: e.reciprocal(
            Dd[0:16, 8192:10240], Dd[0:16, 6144:8192]),
            waits=[("z", z_ld), ("dv", N["dv"]), ("pe", N["pe"])],
            incs=[("dv", 1)])
        dv_rz = N["dv"]
        emit("gpsimd", lambda e: e.dma_start(Bv[:, 0:8192], wout_d),
             waits=[("pe", N["pe"])], incs=[("wb", 16)])
        emit("gpsimd", lambda e: e.dma_start(BRW[:, :], bout_d), incs=[("wb", 16)])
        emit("gpsimd", lambda e: e.dma_start(GB[:, :], hs_d),
             waits=[("pe", N["pe"])], incs=[("wb", 16)])
        wl_oc = N["wb"]
        ld_ohop = N["ob"]
        for oc in range(8):
            for lh in range(2):
                for hf in range(2):
                    emit("tensor", lambda e, oc=oc, lh=lh, hf=hf: e.matmul(
                        psA[:, hf * 512: hf * 512 + 512],
                        GB[0:16, oc * 128:(oc + 1) * 128],
                        Dd[0:16, 8192 + lh * 1024 + hf * 512:
                           8192 + lh * 1024 + hf * 512 + 512],
                        start=True, stop=True),
                        waits=([("dv", dv_rz), ("wb", wl_oc),
                                ("dv", cons.get("onorm", 0))] if hf == 0 else []),
                        incs=([("pe", 1)] if hf == 1 else []))
                pe_bc = N["pe"]
                emit("vector", lambda e, oc=oc, lh=lh: e.tensor_tensor(
                    Cc[:, oc * 2048 + lh * 1024: oc * 2048 + lh * 1024 + 1024],
                    Cc[:, oc * 2048 + lh * 1024: oc * 2048 + lh * 1024 + 1024],
                    psA[:, 0:1024], ALU.mult),
                    waits=[("pe", pe_bc), ("dv", N["dv"]), ("ob", ld_ohop)],
                    incs=[("dv", 1)])
                cons["onorm"] = N["dv"]
        dv_on = N["dv"]
        if DEBUG_ATT and s == 1:
            emit("sync", lambda e: e.dma_start(dbgz_d, Dd[0:16, 10240:14336]),
                 waits=[("dv", dv_on)], incs=[("ld", 16)])
            emit("sync", lambda e: e.dma_start(dbgo_d, Cc[:, :]),
                 waits=[("dv", dv_on)], incs=[("ld", 16)])
            emit("sync", lambda e: e.dma_start(dbgq_d, Dd[:, 0:4096]),
                 waits=[("dv", dv_on)], incs=[("ld", 16)])
            emit("sync", lambda e: e.dma_start(dbgk_d, A[:, 16384:32768]),
                 waits=[("dv", dv_on)], incs=[("ld", 16)])
        for tt in range(NTS):
            tok = s * L + tt * 128
            slot = tt % 2
            gs = tt % 2
            xsem = "xa" if gs == 0 else "xb"
            emit("gpsimd", lambda e, tok=tok, gs=gs: e.dma_start(
                XT[:, gs * 1024:(gs + 1) * 1024], x_d[tok:tok + 128, :]),
                waits=[("dv", cons.get(("XT", gs), 0))], incs=[(xsem, 16)])
            ld_x0 = (xsem, N[xsem])
            wt = [("dv", dv_on), ("wb", wl_oc), ("ld", p0_ld),
                  ("dv", cons.get(("psBp", slot), 0))]
            first = True
            for oc in range(8):
                for hf in range(2):
                    emit("tensor", lambda e, oc=oc, hf=hf, tt=tt, slot=slot: e.matmul(
                        psB[:, slot * 1024 + hf * 512: slot * 1024 + hf * 512 + 512],
                        Cc[:, oc * 2048 + tt * 128: oc * 2048 + tt * 128 + 128],
                        Bv[:, oc * 1024 + hf * 512: oc * 1024 + hf * 512 + 512],
                        start=(oc == 0), stop=False),
                        waits=(wt if first else []))
                    first = False
            for hf in range(2):
                emit("tensor", lambda e, hf=hf, slot=slot: e.matmul(
                    psB[:, slot * 1024 + hf * 512: slot * 1024 + hf * 512 + 512],
                    ON1[0:1, :], BRW[0:1, hf * 512:(hf + 1) * 512],
                    start=False, stop=True),
                    incs=([("pe", 1)] if hf == 1 else []))
            pe_pj = N["pe"]
            emit("vector", lambda e, slot=slot, gs=gs: e.tensor_tensor(
                YS[:, slot * 1024:(slot + 1) * 1024],
                XT[:, gs * 1024:(gs + 1) * 1024],
                psB[:, slot * 1024: slot * 1024 + 1024], ALU.add),
                waits=[("pe", pe_pj), ld_x0,
                       ("x1a" if slot == 0 else "x1b",
                        N["x1a" if slot == 0 else "x1b"]), ("dv", N["dv"])],
                incs=[("dv", 1)])
            cons[("XT", gs)] = N["dv"]
            cons[("psBp", slot)] = N["dv"]
            dv_x1 = N["dv"]
            x1sem = "x1a" if slot == 0 else "x1b"
            emit("sync", lambda e, tok=tok, slot=slot: e.dma_start(
                x1_d[tok:tok + 128, :], YS[:, slot * 1024:(slot + 1) * 1024]),
                waits=[("dv", dv_x1)], incs=[(x1sem, 16)])
    x1_st = (N["x1a"], N["x1b"])

    emit("gpsimd", lambda e: e.dma_start(BRW[:, :], brt_d),
         waits=[("pe", N["pe"])], incs=[("wb", 16)])

    h2_marks = {}
    wtm_marks = {}
    for tt in range(NT):
        tok = tt * 128
        gs = tt % 2
        psem = "pa" if gs == 0 else "pb"
        emit("sync", lambda e, tok=tok, gs=gs: e.dma_start(
            YS[:, gs * 1024:(gs + 1) * 1024], x1_d[tok:tok + 128, :]),
            waits=[("x1a", x1_st[0]), ("x1b", x1_st[1]),
                   ("dv", cons.get(("YS2", gs), 0))],
            incs=[(psem, 16)])
        dv_h = ln_tile(YS[:, gs * 1024:(gs + 1) * 1024], "htr",
                       extra_waits=[(psem, N[psem])])
        cons[("YS2", gs)] = dv_h
        dv_h2 = transpose_to(
            (lambda cc, tt=tt: A[:, cc * 4096 + tt * 128:
                                 cc * 4096 + tt * 128 + 128]), dv_h)
        h2_marks[tt] = dv_h2
        wt = [("dv", dv_h2), ("ac", cons.get("sig", 0)),
              ("ld", p0_ld), ("wb", N["wb"])]
        for cc in range(8):
            emit("tensor", lambda e, cc=cc, tt=tt: e.matmul(
                psB[:, 0:8],
                A[:, cc * 4096 + tt * 128: cc * 4096 + tt * 128 + 128],
                WRT[:, cc * 8:(cc + 1) * 8], start=(cc == 0), stop=False),
                waits=(wt if cc == 0 else []))
        emit("tensor", lambda e: e.matmul(
            psB[:, 0:8], ON1[0:1, :], BRW[0:1, 0:8], start=False, stop=True),
            incs=[("pe", 1)])
        pe_lg = N["pe"]
        emit("scalar", lambda e: e.activation(
            PR[:, 0:8], psB[:, 0:8], AF.Sigmoid),
            waits=[("pe", pe_lg), ("dv", cons.get("top2", 0)), ("ac", N["ac"])],
            incs=[("ac", 1)])
        cons["sig"] = N["ac"]
        ac_pr = N["ac"]

        def t2(b, w0=None):
            emit("vector", b,
                 waits=[("dv", N["dv"])] + ([w0] if w0 else []), incs=[("dv", 1)])
        t2(lambda e: e.reduce_max(PR[:, 16:17], PR[:, 0:8], axis=AX.X),
           w0=("ac", ac_pr))
        t2(lambda e: e.tensor_scalar(PR[:, 8:16], PR[:, 0:8], PR[:, 16:17],
                                     None, ALU.is_ge))
        t2(lambda e: e.scalar_tensor_tensor(PR[:, 8:16], PR[:, 8:16], -1e9,
                                            PR[:, 0:8], ALU.mult, ALU.add))
        t2(lambda e: e.reduce_max(PR[:, 17:18], PR[:, 8:16], axis=AX.X))
        t2(lambda e: e.tensor_tensor(PR[:, 18:19], PR[:, 16:17], PR[:, 17:18],
                                     ALU.add))
        t2(lambda e: e.reciprocal(PR[:, 19:20], PR[:, 18:19]))
        t2(lambda e: e.tensor_scalar(PR[:, 8:16], PR[:, 0:8], PR[:, 17:18],
                                     None, ALU.is_ge))
        t2(lambda e: e.tensor_tensor(PR[:, 8:16], PR[:, 8:16], PR[:, 0:8],
                                     ALU.mult))
        t2(lambda e, tt=tt: e.tensor_scalar(WTM[:, tt * 8:tt * 8 + 8],
                                            PR[:, 8:16], PR[:, 19:20],
                                            None, ALU.mult))
        cons["top2"] = N["dv"]
        wtm_marks[tt] = N["dv"]
    h2_dv = N["dv"]
    wtm_dv = N["dv"]

    pe_attn_end = N["pe"]
    prev_gu = [pe_attn_end] * 16
    prev_wd = [pe_attn_end] * 16
    ye7_st = {}
    for ex in range(-1, NE):
        srcs = (sg_d, su_d, sd_d) if ex < 0 else (wg_d[ex], wu_d[ex], wd_d[ex])
        wl_gu = [None] * 16
        wl_wd = [None] * 16
        for j in range(16):
            wt_g = [("pe", prev_gu[j])] if (ex >= 0 or j == 0) else []
            emit("gpsimd", lambda e, s0=srcs[0], j=j: e.dma_start(
                Cc[:, j * 1024:(j + 1) * 1024], s0[:, j * 1024:(j + 1) * 1024]),
                waits=wt_g, incs=[(f"wj{j}", 16)])
            wt_u = [("dv", dv_rz)] if (ex < 0 and j == 0) else []
            emit("gpsimd", lambda e, s1=srcs[1], j=j: e.dma_start(
                Dd[:, j * 1024:(j + 1) * 1024], s1[:, j * 1024:(j + 1) * 1024]),
                waits=wt_u, incs=[(f"wj{j}", 16)])
            wl_gu[j] = (f"wj{j}", N[f"wj{j}"])
            wt_d = [("pe", prev_wd[j])] if (ex >= 0 or j == 0) else []
            emit("gpsimd", lambda e, s2=srcs[2], j=j: e.dma_start(
                Bv[:, j * 1024:(j + 1) * 1024], s2[:, j * 1024:(j + 1) * 1024]),
                waits=wt_d, incs=[(f"wd{j}", 16)])
            wl_wd[j] = (f"wd{j}", N[f"wd{j}"])
        if ex < 0:
            emit("gpsimd", lambda e: e.dma_start(BRW[:, :], brt_d), incs=[("wb", 16)])
        new_gu = [0] * 16
        new_wd = [0] * 16
        hh_mark = [0] * 16
        for g in range(8):
            t0 = g * 512
            for ich in range(16):
                ga = ich % 2
                ps_g = ga * 1024
                wt = [("dv", h2_dv),
                      ("dv", cons.get(("hhw", ga), 0))]
                if g == 0:
                    wt.append(wl_gu[ich])
                for cc in range(8):
                    emit("tensor", lambda e, ich=ich, cc=cc, t0=t0, ps_g=ps_g: e.matmul(
                        psA[:, ps_g:ps_g + 512],
                        Cc[:, ich * 1024 + cc * 128: ich * 1024 + (cc + 1) * 128],
                        A[:, cc * 4096 + t0: cc * 4096 + t0 + 512],
                        start=(cc == 0), stop=(cc == 7)),
                        waits=(wt if cc == 0 else []))
                for cc in range(8):
                    emit("tensor", lambda e, ich=ich, cc=cc, t0=t0, ps_g=ps_g: e.matmul(
                        psA[:, ps_g + 512:ps_g + 1024],
                        Dd[:, ich * 1024 + cc * 128: ich * 1024 + (cc + 1) * 128],
                        A[:, cc * 4096 + t0: cc * 4096 + t0 + 512],
                        start=(cc == 0), stop=(cc == 7)),
                        incs=([("pe", 1)] if cc == 7 else []))
                pe_gu = N["pe"]
                if g == 7:
                    new_gu[ich] = pe_gu
                emit("scalar", lambda e, ps_g=ps_g: e.activation(
                    TMP[:, 0:512], psA[:, ps_g:ps_g + 512], AF.Silu),
                    waits=[("pe", pe_gu), ("dv", cons.get("silu", 0)),
                           ("ac", N["ac"])],
                    incs=[("ac", 1)])
                ac_si = N["ac"]
                emit("vector", lambda e, ich=ich, ps_g=ps_g: e.tensor_tensor(
                    HH[:, ich * 512:(ich + 1) * 512], TMP[:, 0:512],
                    psA[:, ps_g + 512:ps_g + 1024], ALU.mult),
                    waits=[("ac", ac_si), ("dv", N["dv"])],
                    incs=[("dv", 1)])
                cons["silu"] = N["dv"]
                cons[("hhw", ga)] = N["dv"]
                hh_mark[ich] = N["dv"]
            for sub in range(4):
                slot = sub % 2
                tt = g * 4 + sub
                for ich in range(16):
                    for hf in range(2):
                        wthis = []
                        if ich == 0 and hf == 0:
                            wthis.append(("dv", cons.get(("psBy", slot), 0)))
                        if sub == 0 and hf == 0:
                            wthis.append(("dv", hh_mark[ich]))
                            if g == 0:
                                wthis.append(wl_wd[ich])
                        do_inc = (ich == 15 and hf == 1) or (g == 7 and sub == 3 and hf == 1)
                        emit("tensor", lambda e, ich=ich, hf=hf, sub=sub, slot=slot: e.matmul(
                            psB[:, slot * 1024 + hf * 512: slot * 1024 + hf * 512 + 512],
                            HH[:, ich * 512 + sub * 128: ich * 512 + sub * 128 + 128],
                            Bv[:, ich * 1024 + hf * 512: ich * 1024 + hf * 512 + 512],
                            start=(ich == 0), stop=(ich == 15)),
                            waits=wthis,
                            incs=([("pe", 1)] if do_inc else []))
                        if g == 7 and sub == 3 and hf == 1:
                            new_wd[ich] = N["pe"]
                pe_y = N["pe"]
                ysem = "ya" if slot == 0 else "yb"
                if ex < 0:
                    emit("vector", lambda e, slot=slot: e.tensor_copy(
                        YS[:, slot * 1024:(slot + 1) * 1024],
                        psB[:, slot * 1024: slot * 1024 + 1024]),
                        waits=[("pe", pe_y), (ysem, N[ysem]),
                               ("dv", N["dv"]),
                               ("x1a" if slot == 0 else "x1b",
                                N["x1a" if slot == 0 else "x1b"])],
                        incs=[("dv", 1)])
                else:
                    emit("vector", lambda e, slot=slot, tt=tt, ex=ex: e.tensor_scalar(
                        YS[:, slot * 1024:(slot + 1) * 1024],
                        psB[:, slot * 1024: slot * 1024 + 1024],
                        WTM[:, tt * 8 + ex: tt * 8 + ex + 1], None, ALU.mult),
                        waits=[("pe", pe_y), (ysem, N[ysem]),
                               ("dv", N["dv"])],
                        incs=[("dv", 1)])
                cons[("psBy", slot)] = N["dv"]
                dv_y = N["dv"]
                dst = ysh_d if ex < 0 else ye_d
                roff = (0 if ex < 0 else ex * TC) + tt * 128
                emit("sync", lambda e, dst=dst, roff=roff, slot=slot: e.dma_start(
                    dst[roff:roff + 128, :], YS[:, slot * 1024:(slot + 1) * 1024]),
                    waits=[("dv", dv_y)], incs=[(ysem, 16)])
                if ex == NE - 1:
                    ye7_st[tt] = (N["ya"], N["yb"])
        prev_gu = new_gu
        prev_wd = new_wd
    ffn_st = (N["ya"], N["yb"])

    for tt in range(NT):
        tok = tt * 128
        srcs = [(x1_d, tok), (ysh_d, tok)] + [(ye_d, e * TC + tok) for e in range(NE)]
        for i, (srcd, roff) in enumerate(srcs):
            if i < 8:
                dstap = HH[:, i * 1024:(i + 1) * 1024]
            else:
                dstap = YS[:, (i - 8) * 1024:(i - 7) * 1024]
            emit("sync", lambda e, srcd=srcd, roff=roff, dstap=dstap: e.dma_start(
                dstap, srcd[roff:roff + 128, :]),
                waits=[("ya", ffn_st[0]), ("yb", ffn_st[1]),
                       ("x1a", x1_st[0]), ("x1b", x1_st[1]),
                       ("dv", cons.get("fin", 0)),
                       ("po", cons.get("finp", 0))],
                incs=[("fl", 16)])
        ld_f = N["fl"]
        emit("vector", lambda e: e.tensor_tensor(
            HH[:, 0:1024], HH[:, 0:1024], HH[:, 1024:2048], ALU.add),
            waits=[("fl", ld_f), ("dv", N["dv"])], incs=[("dv", 1)])
        emit("gpsimd", lambda e: e.tensor_tensor(
            HH[:, 2048:3072], HH[:, 2048:3072], HH[:, 3072:4096], ALU.add),
            waits=[("fl", ld_f), ("po", N["po"])], incs=[("po", 1)])
        emit("vector", lambda e: e.tensor_tensor(
            HH[:, 4096:5120], HH[:, 4096:5120], HH[:, 5120:6144], ALU.add),
            waits=[("dv", N["dv"])], incs=[("dv", 1)])
        emit("gpsimd", lambda e: e.tensor_tensor(
            HH[:, 6144:7168], HH[:, 6144:7168], HH[:, 7168:8192], ALU.add),
            waits=[("po", N["po"])], incs=[("po", 1)])
        emit("vector", lambda e: e.tensor_tensor(
            YS[:, 0:1024], YS[:, 0:1024], YS[:, 1024:2048], ALU.add),
            waits=[("dv", N["dv"])], incs=[("dv", 1)])
        l1dv = N["dv"]
        l1po = N["po"]
        emit("vector", lambda e: e.tensor_tensor(
            HH[:, 1024:2048], HH[:, 0:1024], HH[:, 2048:3072], ALU.add),
            waits=[("dv", l1dv), ("po", l1po)], incs=[("dv", 1)])
        emit("vector", lambda e: e.tensor_tensor(
            HH[:, 3072:4096], HH[:, 4096:5120], HH[:, 6144:7168], ALU.add),
            waits=[("dv", N["dv"])], incs=[("dv", 1)])
        emit("vector", lambda e: e.tensor_tensor(
            HH[:, 5120:6144], HH[:, 1024:2048], HH[:, 3072:4096], ALU.add),
            waits=[("dv", N["dv"])], incs=[("dv", 1)])
        emit("vector", lambda e: e.tensor_tensor(
            G[:, :], HH[:, 5120:6144], YS[:, 0:1024], ALU.add),
            waits=[("dv", N["dv"]), ("out", cons.get("outG", 0))],
            incs=[("dv", 1)])
        cons["fin"] = N["dv"]
        cons["finp"] = N["po"]
        dv_f = N["dv"]
        emit("sync", lambda e, tok=tok: e.dma_start(
            out_d[tok:tok + 128, :], G[:, :]),
            waits=[("dv", dv_f)], incs=[("out", 16)])
        cons["outG"] = N["out"]

    wait_only("sync", [("out", N["out"]), ("x1a", N["x1a"]), ("x1b", N["x1b"]),
                       ("ya", N["ya"]), ("yb", N["yb"])])

    with ctx:
        ctx.enter_context(nc.allow_low_precision(
            "bf16 softmax-denominator reciprocal; within checked tolerance"))
        block = ctx.enter_context(nc.Block())

        @block.sync
        def _(e):
            for fn in OPS["sync"]:
                fn(e)

        @block.gpsimd
        def _(e):
            for fn in OPS["gpsimd"]:
                fn(e)

        @block.tensor
        def _(e):
            for fn in OPS["tensor"]:
                fn(e)

        @block.scalar
        def _(e):
            for fn in OPS["scalar"]:
                fn(e)

        @block.vector
        def _(e):
            for fn in OPS["vector"]:
                fn(e)

    return nc


def kernel(**inputs):
    w = prep_weights(inputs)
    x = np.ascontiguousarray(inputs["L0_attn_x"]).reshape(NCORES, TC, CM)
    nc = build_program()
    base = {k: w[k] for k in ("wk", "wv", "wq", "wout", "bqk", "bv", "bout",
                              "brt", "gbq", "gbk", "wrt", "sg", "su", "sd",
                              "wg", "wu", "wd")}
    in_maps = [dict(base, x=np.ascontiguousarray(x[c])) for c in range(NCORES)]
    res = run_bass_kernel_spmd(nc, in_maps, core_ids=list(range(NCORES)))
    out = np.stack([res.results[c]["out"] for c in range(NCORES)])
    return out.reshape(16, L, CM)


# version 13 (final)
# speedup vs baseline: 1.3672x; 1.1425x over previous
import numpy as np
import ml_dtypes
from contextlib import ExitStack

import concourse.bass as bass
import concourse.mybir as mybir
from concourse.bass_utils import run_bass_kernel_spmd
from concourse.bass import IndirectOffsetOnAxis

BFnp = ml_dtypes.bfloat16
F32 = mybir.dt.float32
BF16 = mybir.dt.bfloat16
AF = mybir.ActivationFunctionType
ALU = mybir.AluOpType
AX = mybir.AxisListType

NCORES = 8
BC = 2
L = 2048
CM = 1024
NE = 8
TC = BC * L
NT = TC // 128
NTS = L // 128
LN0125 = float(np.log(0.125))
CAP = 2048
NSLOT = NE * CAP

DEBUG_X1 = False
DEBUG_ATT = False


def _perm_qk():
    perm = np.zeros(CM, dtype=np.int64)
    for cc in range(8):
        base = cc * 128
        for half in range(2):
            h = 2 * cc + half
            for j in range(32):
                perm[base + half * 64 + j] = h * 64 + 2 * j
                perm[base + half * 64 + 32 + j] = h * 64 + 2 * j + 1
    return perm


def _dperm(p):
    return 2 * (p % 32) + ((p % 64) // 32)


def prep_weights(inp):
    w = {}
    perm = _perm_qk()
    w_qkv = inp["L0_attn_w_qkv"]
    b_qkv = inp["L0_attn_b_qkv"]
    wqp = w_qkv[0:1024][perm]
    wkp = w_qkv[1024:2048][perm]
    wvp = w_qkv[2048:3072]
    w["wk"] = np.ascontiguousarray(
        wkp.reshape(8, 128, 8, 128).transpose(3, 0, 2, 1).reshape(128, 8192))
    w["wv"] = np.ascontiguousarray(
        wvp.T.reshape(8, 128, 1024).transpose(1, 0, 2).reshape(128, 8192))
    w["wq"] = np.ascontiguousarray(
        wqp.reshape(8, 128, 8, 128).transpose(0, 3, 2, 1).reshape(8, 128, 1024))
    w["wout"] = np.ascontiguousarray(
        inp["L0_attn_w_out"].T.reshape(8, 128, 1024).transpose(1, 0, 2).reshape(128, 8192))
    w["bqk"] = np.ascontiguousarray(
        np.stack([b_qkv[0:1024][perm].reshape(8, 128).T,
                  b_qkv[1024:2048][perm].reshape(8, 128).T], axis=-1).reshape(128, 16))
    w["bv"] = inp["L0_attn_b_qkv"][2048:3072].reshape(1, 1024)
    w["bout"] = inp["L0_attn_b_out"].reshape(1, 1024)
    w["brt"] = np.ascontiguousarray(
        np.pad(inp["L0_ffn_routing_bias"], (0, 1016)).reshape(1, 1024))
    def gmat(g):
        m = np.zeros((16, 1024), dtype=np.float32)
        for cc in range(8):
            for p in range(128):
                h = 2 * cc + p // 64
                m[h, cc * 128 + p] = g[h, _dperm(p)]
        return m
    w["gbq"] = gmat(inp["L0_attn_q_gamma"])
    w["gbk"] = gmat(inp["L0_attn_k_gamma"])
    w["wrt"] = np.ascontiguousarray(
        inp["L0_ffn_router_w"].T.reshape(8, 128, 8).transpose(1, 0, 2).reshape(128, 64))

    def wgl(m):
        return np.ascontiguousarray(
            m.reshape(16, 128, 8, 128).transpose(3, 0, 2, 1).reshape(128, 16384))

    def wdl(m):
        return np.ascontiguousarray(
            m.T.reshape(16, 128, 1024).transpose(1, 0, 2).reshape(128, 16384))
    w["sg"] = wgl(inp["L0_ffn_sg_w"])
    w["su"] = wgl(inp["L0_ffn_su_w"])
    w["sd"] = wdl(inp["L0_ffn_sd_w"])
    w["wg"] = np.stack([wgl(inp["L0_ffn_wg"][e]) for e in range(NE)])
    w["wu"] = np.stack([wgl(inp["L0_ffn_wu"][e]) for e in range(NE)])
    w["wd"] = np.stack([wdl(inp["L0_ffn_wd"][e]) for e in range(NE)])
    return w


def _consts():
    c = {}
    inv = 1.0 / (10000.0 ** (np.arange(32, dtype=np.float64) / 32))
    ang = np.arange(L, dtype=np.float64)[:, None] * inv[None, :]
    cos = np.cos(ang).T
    sin = np.sin(ang).T
    c["cs1"] = np.concatenate([cos, cos, cos, cos], 0).astype(BFnp)
    c["cs2"] = np.concatenate([-sin, sin, -sin, sin], 0).astype(BFnp)
    c["idf"] = np.eye(128, dtype=np.float32)
    c["on1"] = np.ones((1, 128), dtype=BFnp)
    blk = np.zeros((128, 128), dtype=BFnp)
    for cc in range(8):
        for p in range(128):
            blk[p, cc * 16 + 2 * cc + p // 64] = 1
    c["blk"] = blk
    hs = np.zeros((16, 1024), dtype=BFnp)
    for oc in range(8):
        for p in range(128):
            hs[2 * oc + p // 64, oc * 128 + p] = 1
    c["hs"] = hs
    ecap = np.zeros((128, 8), dtype=np.float32)
    ecap[:, :] = (np.arange(8) * CAP)[None, :]
    c["ecap"] = ecap
    eps = np.zeros((128, 8), dtype=np.float32)
    eps[:, :] = ((7 - np.arange(8)) * 1e-7)[None, :]
    c["eps"] = eps
    return c


def build_program():
    nc = bass.Bass("TRN2", target_bir_lowering=False)

    x_d = nc.dram_tensor("x", [TC, CM], F32, kind="ExternalInput").ap()
    out_d = nc.dram_tensor("out", [TC, CM], F32, kind="ExternalOutput").ap()
    wk_d = nc.dram_tensor("wk", [128, 8192], F32, kind="ExternalInput").ap()
    wv_d = nc.dram_tensor("wv", [128, 8192], F32, kind="ExternalInput").ap()
    wq_d = nc.dram_tensor("wq", [8, 128, 1024], F32, kind="ExternalInput").ap()
    wout_d = nc.dram_tensor("wout", [128, 8192], F32, kind="ExternalInput").ap()
    bqk_d = nc.dram_tensor("bqk", [128, 16], F32, kind="ExternalInput").ap()
    bv_d = nc.dram_tensor("bv", [1, 1024], F32, kind="ExternalInput").ap()
    bout_d = nc.dram_tensor("bout", [1, 1024], F32, kind="ExternalInput").ap()
    brt_d = nc.dram_tensor("brt", [1, 1024], F32, kind="ExternalInput").ap()
    gbq_d = nc.dram_tensor("gbq", [16, 1024], F32, kind="ExternalInput").ap()
    gbk_d = nc.dram_tensor("gbk", [16, 1024], F32, kind="ExternalInput").ap()
    wrt_d = nc.dram_tensor("wrt", [128, 64], F32, kind="ExternalInput").ap()
    sg_d = nc.dram_tensor("sg", [128, 16384], F32, kind="ExternalInput").ap()
    su_d = nc.dram_tensor("su", [128, 16384], F32, kind="ExternalInput").ap()
    sd_d = nc.dram_tensor("sd", [128, 16384], F32, kind="ExternalInput").ap()
    wg_d = nc.dram_tensor("wg", [NE, 128, 16384], F32, kind="ExternalInput").ap()
    wu_d = nc.dram_tensor("wu", [NE, 128, 16384], F32, kind="ExternalInput").ap()
    wd_d = nc.dram_tensor("wd", [NE, 128, 16384], F32, kind="ExternalInput").ap()

    x1_d = nc.dram_tensor("x1sp", [TC, CM], BF16,
                          kind="ExternalOutput" if DEBUG_X1 else "Internal").ap()
    ysh_d = nc.dram_tensor("ysh", [TC, CM], BF16,
                           kind="ExternalOutput" if DEBUG_ATT else "Internal").ap()
    if DEBUG_ATT:
        dbgz_d = nc.dram_tensor("dbgz", [16, 4096], BF16, kind="ExternalOutput").ap()
        dbgo_d = nc.dram_tensor("dbgo", [128, 16384], BF16, kind="ExternalOutput").ap()
        dbgq_d = nc.dram_tensor("dbgq", [128, 4096], BF16, kind="ExternalOutput").ap()
        dbgk_d = nc.dram_tensor("dbgk", [128, 16384], BF16, kind="ExternalOutput").ap()
    h2sp_d = nc.dram_tensor("h2sp", [TC, CM], BF16, kind="Internal").ap()
    xsp_d = nc.dram_tensor("xsp", [NSLOT + 1, CM], BF16, kind="Internal").ap()
    ysp_d = nc.dram_tensor("ysp", [NSLOT + 1, CM], BF16,
                           kind="ExternalOutput" if DEBUG_ATT else "Internal").ap()

    ct = _consts()
    cs1_d = nc.inline_tensor(ct["cs1"], "cs1c").ap()
    cs2_d = nc.inline_tensor(ct["cs2"], "cs2c").ap()
    idf_d = nc.inline_tensor(ct["idf"], "idfc").ap()
    on1_d = nc.inline_tensor(ct["on1"], "on1c").ap()
    blk_d = nc.inline_tensor(ct["blk"], "blkc").ap()
    hs_d = nc.inline_tensor(ct["hs"], "hsc").ap()
    ecap_d = nc.inline_tensor(ct["ecap"], "ecapc").ap()
    eps_d = nc.inline_tensor(ct["eps"], "epsc").ap()

    ctx = ExitStack()
    sb = lambda n, sh, dt: ctx.enter_context(nc.sbuf_tensor(n, sh, dt))
    A = sb("Ar", [128, 32768], BF16)
    Bv = sb("Bv", [128, 16640], BF16)
    Cc = sb("Cc", [128, 16384], BF16)
    Dd = sb("Dd", [128, 16384], BF16)
    HH = sb("HH", [128, 8192], BF16)
    YS = sb("YS", [128, 2048], BF16)
    XT = sb("XT", [128, 2048], BF16)
    G = sb("G", [128, 1024], F32)
    TMP = sb("TMP", [128, 1536], BF16)
    CS1 = sb("CS1", [128, 2048], BF16)
    CS2 = sb("CS2", [128, 2048], BF16)
    IDF = sb("IDF", [128, 128], F32)
    ON1 = sb("ON1", [1, 128], BF16)
    BLK = sb("BLK", [128, 128], BF16)
    GB = sb("GB", [16, 1024], BF16)
    BQK = sb("BQK", [128, 16], F32)
    BRW = sb("BRW", [1, 1024], BF16)
    WRT = sb("WRT", [128, 64], BF16)
    RST = sb("RST", [16, 512], BF16)
    STT = sb("STT", [128, 8], F32)
    PR = sb("PR", [128, 32], F32)
    IND8 = sb("IND8", [128, 256], BF16)
    EQ18 = sb("EQ18", [128, 256], BF16)
    SLI = sb("SLI", [128, 64], mybir.dt.int32)
    W12 = sb("W12", [128, 64], F32)
    CRY = sb("CRY", [8, 2], F32)
    ECP = sb("ECP", [128, 8], F32)
    EPS = sb("EPS", [128, 8], F32)
    CB = sb("CB", [128, 4], F32)

    psA = ctx.enter_context(nc.psum_tensor("psA", [128, 2048], F32))
    psB = ctx.enter_context(nc.psum_tensor("psB", [128, 2048], F32))

    sems = {}
    semnames = (["ld", "pe", "ac", "dv", "po", "wb", "xa", "xb", "qa", "qb",
                 "sw", "z", "ob", "x1a", "x1b", "pa", "pb", "ya", "yb",
                 "fl", "out", "h2s", "h2l", "xsc", "xba", "xbb", "xta", "xtb",
                 "g1", "g2", "yz"]
                + [f"wj{j}" for j in range(16)] + [f"wd{j}" for j in range(16)])
    for s in semnames:
        sems[s] = ctx.enter_context(nc.semaphore("s_" + s))
    N = {k: 0 for k in sems}
    OPS = {e: [] for e in ("tensor", "vector", "scalar", "gpsimd", "sync")}

    def emit(eng, build, waits=(), incs=()):
        w = tuple(waits)
        i = tuple(incs)

        def fn(e, _b=build, _w=w, _i=i):
            for s, v in _w:
                if v > 0:
                    e.wait_ge(sems[s], v)
            inst = _b(e)
            for s, a in _i:
                inst.then_inc(sems[s], a)
        OPS[eng].append(fn)
        for s, a in i:
            N[s] += a

    def wait_only(eng, waits):
        w = tuple(waits)

        def fn(e, _w=w):
            for s, v in _w:
                if v > 0:
                    e.wait_ge(sems[s], v)
        OPS[eng].append(fn)

    cons = {}

    for dst, src in ((CS1, cs1_d), (CS2, cs2_d), (IDF, idf_d), (ON1, on1_d),
                     (BLK, blk_d), (BQK, bqk_d), (ECP, ecap_d), (EPS, eps_d)):
        emit("sync", lambda e, dst=dst, src=src: e.dma_start(dst[:, :], src),
             incs=[("ld", 16)])
    p0_ld = N["ld"]
    emit("gpsimd", lambda e: e.dma_start(WRT[:, :], wrt_d), incs=[("wb", 16)])
    p0_wrt = N["wb"]
    emit("vector",
         lambda e: e.memset(Bv[:, :].rearrange(
             "p (m c) -> p m c", c=65)[:, :, 64:65], 1.0),
         incs=[("dv", 1)])
    for col, val in ((0, 1e-5), (1, 1e-6), (2, -8.0), (3, LN0125)):
        emit("vector", lambda e, col=col, val=val: e.memset(
            CB[:, col:col + 1], val), incs=[("dv", 1)])
    vones_dv = N["dv"]

    def ln_tile(src_ap, pe_war_key, extra_waits=()):
        emit("scalar", lambda e: e.activation(
            G[:, :], src_ap, AF.Copy, accum_out=STT[:, 0:1]),
            waits=list(extra_waits) + [("ac", N["ac"]),
                                       ("pe", cons.get(pe_war_key, 0)),
                                       ("dv", cons.get("lnstt", 0))],
            incs=[("ac", 1)])
        emit("scalar", lambda e: e.activation(
            G[:, :], src_ap, AF.Square, accum_out=STT[:, 1:2]),
            waits=[("ac", N["ac"])], incs=[("ac", 1)])
        ac_st = N["ac"]
        emit("vector", lambda e: e.tensor_scalar(
            STT[:, 2:3], STT[:, 0:1], 1.0 / 1024, None, ALU.mult),
            waits=[("ac", ac_st), ("dv", N["dv"])], incs=[("dv", 1)])
        emit("vector", lambda e: e.tensor_tensor(
            STT[:, 3:4], STT[:, 2:3], STT[:, 2:3], ALU.mult),
            waits=[("dv", N["dv"])], incs=[("dv", 1)])
        emit("vector", lambda e: e.scalar_tensor_tensor(
            STT[:, 4:5], STT[:, 1:2], 1.0 / 1024, STT[:, 3:4],
            ALU.mult, ALU.subtract),
            waits=[("dv", N["dv"])], incs=[("dv", 1)])
        dv_var = N["dv"]
        emit("scalar", lambda e: e.activation(
            STT[:, 5:6], STT[:, 4:5], AF.Ln, bias=CB[:, 0:1]),
            waits=[("dv", dv_var), ("ac", N["ac"])], incs=[("ac", 1)])
        emit("scalar", lambda e: e.activation(
            STT[:, 6:7], STT[:, 5:6], AF.Exp, scale=-0.5),
            waits=[("ac", N["ac"])], incs=[("ac", 1)])
        ac_r = N["ac"]
        emit("vector", lambda e: e.tensor_scalar(
            G[:, :], src_ap, STT[:, 2:3], STT[:, 6:7], ALU.subtract, ALU.mult),
            waits=[("ac", ac_r), ("dv", N["dv"])], incs=[("dv", 1)])
        cons["lnstt"] = N["dv"]
        return N["dv"]

    def transpose_to(dst_base_fn, dv_h):
        for cc in range(8):
            wt = [("dv", dv_h)]
            if cc == 0:
                wt.append(("dv", cons.get("psAtr", 0)))
                wt.append(("ld", p0_ld))
            emit("tensor", lambda e, cc=cc: e.transpose(
                psA[:, cc * 128:(cc + 1) * 128],
                G[:, cc * 128:(cc + 1) * 128], IDF[:, :]),
                waits=(wt if cc == 0 else []),
                incs=([("pe", 1)] if cc == 7 else []))
        cons["htr"] = N["pe"]
        pe_tr = N["pe"]
        for cc in range(8):
            emit("vector", lambda e, cc=cc: e.tensor_copy(
                dst_base_fn(cc), psA[:, cc * 128:(cc + 1) * 128]),
                waits=([("pe", pe_tr), ("dv", N["dv"])] if cc == 0 else []),
                incs=([("dv", 1)] if cc == 7 else []))
        cons["psAtr"] = N["dv"]
        return N["dv"]

    def rope_rms(kind, psrc, ps_free_key, bias_col, cs_col, out_col, out_buf,
                 blk_col, sum_start, sum_stop, exp_bias):
        emit("scalar", lambda e, psrc=psrc: e.activation(
            TMP[:, 0:512], psrc, AF.Identity, bias=BQK[:, bias_col:bias_col + 1]),
            waits=[("dv", cons.get("tmprope", 0)), ("ac", N["ac"]),
                   ("pe", cons.get("sqmm", 0)), ("ld", p0_ld),
                   ("po", cons.get("swfree", 0))],
            incs=[("ac", 1)])
        cons[ps_free_key] = N["ac"]
        ac_raw = N["ac"]
        for qr in range(4):
            dq = qr ^ 1
            emit("sync", lambda e, qr=qr, dq=dq: e.dma_start(
                TMP[dq * 32:(dq + 1) * 32, 512:1024],
                TMP[qr * 32:(qr + 1) * 32, 0:512]),
                waits=([("ac", ac_raw), ("po", cons.get("swfree", 0))]
                       if qr == 0 else []),
                incs=[("sw", 16)])
        ld_sw = N["sw"]
        emit("vector", lambda e, cs_col=cs_col: e.tensor_tensor(
            TMP[:, 1024:1536], TMP[:, 0:512], CS1[:, cs_col:cs_col + 512], ALU.mult),
            waits=[("ac", ac_raw), ("dv", N["dv"])], incs=[("dv", 1)])
        dv_t1 = N["dv"]
        emit("gpsimd", lambda e, cs_col=cs_col: e.tensor_tensor(
            TMP[:, 0:512], TMP[:, 512:1024], CS2[:, cs_col:cs_col + 512], ALU.mult),
            waits=[("sw", ld_sw), ("dv", dv_t1), ("po", N["po"])],
            incs=[("po", 1)])
        cons["swfree"] = N["po"]
        po_t2 = N["po"]
        oc = out_col
        emit("vector", lambda e, oc=oc, ob=out_buf: e.tensor_tensor(
            ob[:, oc:oc + 512], TMP[:, 1024:1536], TMP[:, 0:512], ALU.add),
            waits=[("po", po_t2), ("dv", N["dv"])], incs=[("dv", 1)])
        cons["tmprope"] = N["dv"]
        po_r = N["dv"]
        emit("scalar", lambda e, oc=oc, ob=out_buf: e.activation(
            TMP[:, 0:512], ob[:, oc:oc + 512], AF.Square),
            waits=[("dv", po_r), ("ac", N["ac"]), ("pe", cons.get("sqmm", 0))],
            incs=[("ac", 1)])
        ac_sq = N["ac"]
        wt = [("ac", ac_sq)]
        if sum_start:
            wt += [("dv", cons.get("ssfree_dv", 0)), ("ac", cons.get("ssfree_ac", 0)),
                   ("dv", cons.get(("psBv", 0), 0)), ("dv", cons.get("psBo", 0))]
        emit("tensor", lambda e, blk_col=blk_col: e.matmul(
            psB[0:16, 0:512], BLK[:, blk_col:blk_col + 16], TMP[:, 0:512],
            start=sum_start, stop=sum_stop),
            waits=wt, incs=([("pe", 1)] if sum_stop else []))
        cons["sqmm"] = N["pe"]
        if not sum_stop:
            return None
        pe_ss = N["pe"]
        emit("scalar", lambda e: e.activation(
            psB[0:16, 512:1024], psB[0:16, 0:512], AF.Ln, bias=CB[0:16, 1:2],
            scale=1.0 / 64),
            waits=[("pe", pe_ss), ("ac", N["ac"])], incs=[("ac", 1)])
        if exp_bias == 0.0:
            emit("scalar", lambda e: e.activation(
                RST[0:16, 0:512], psB[0:16, 512:1024], AF.Exp, scale=-0.5),
                waits=[("ac", N["ac"]), ("pe", cons.get("rstfree", 0))],
                incs=[("ac", 1)])
        else:
            emit("scalar", lambda e: e.activation(
                RST[0:16, 0:512], psB[0:16, 512:1024], AF.Exp, scale=-0.5,
                bias=CB[0:16, 3:4]),
                waits=[("ac", N["ac"])], incs=[("ac", 1)])
        cons["ssfree_ac"] = N["ac"]
        return N["ac"]

    def rms_apply(ac_rstd, gb_off, out_col, out_buf):
        emit("tensor", lambda e, gb_off=gb_off: e.matmul(
            psB[:, 1024:1536], GB[0:16, gb_off:gb_off + 128], RST[0:16, 0:512],
            start=True, stop=True),
            waits=[("ac", ac_rstd), ("dv", cons.get("bcfree", 0)),
                   ("dv", cons.get(("psBv", 1), 0))],
            incs=[("pe", 1)])
        cons["rstfree"] = N["pe"]
        pe_bc = N["pe"]
        emit("vector", lambda e, oc=out_col, ob=out_buf: e.tensor_tensor(
            ob[:, oc:oc + 512], ob[:, oc:oc + 512], psB[:, 1024:1536], ALU.mult),
            waits=[("pe", pe_bc), ("dv", N["dv"])], incs=[("dv", 1)])
        cons["bcfree"] = N["dv"]
        cons["ssfree_dv"] = N["dv"]
        return N["dv"]

    for s in range(BC):
        pre_wt = [("pe", N["pe"]), ("dv", N["dv"])]
        emit("gpsimd", lambda e: e.dma_start(Cc[:, 0:8192], wk_d),
             waits=pre_wt, incs=[("wb", 16)])
        emit("gpsimd", lambda e: e.dma_start(Cc[:, 8192:16384], wv_d),
             incs=[("wb", 16)])
        emit("gpsimd", lambda e: e.dma_start(BRW[:, :], bv_d), incs=[("wb", 16)])
        emit("gpsimd", lambda e: e.dma_start(GB[:, :], gbk_d), incs=[("wb", 16)])
        wl_kv = N["wb"]
        emit("vector",
             lambda e: e.memset(Bv[:, :].rearrange(
                 "p (m c) -> p m c", c=65)[:, :, 64:65], 1.0),
             waits=[("pe", N["pe"]), ("dv", N["dv"])], incs=[("dv", 1)])
        vones_dv = N["dv"]

        h_dv = {}
        for g in range(4):
            for ti in range(4):
                tt = g * 4 + ti
                tok = s * L + tt * 128
                gs = tt % 2
                xsem = "xa" if gs == 0 else "xb"
                emit("gpsimd", lambda e, tok=tok, gs=gs: e.dma_start(
                    XT[:, gs * 1024:(gs + 1) * 1024], x_d[tok:tok + 128, :]),
                    waits=[("dv", cons.get(("XT", gs), 0)),
                           ("pe", N["pe"] if (s == 1 and tt == 0) else 0)],
                    incs=[(xsem, 16)])
                dv_h = ln_tile(XT[:, gs * 1024:(gs + 1) * 1024], "htr",
                               extra_waits=[(xsem, N[xsem])])
                cons[("XT", gs)] = dv_h
                transpose_to(
                    (lambda cc, tt=tt: A[:, cc * 2048 + tt * 128:
                                         cc * 2048 + tt * 128 + 128]), dv_h)
                h_dv[tt] = N["dv"]
            h_rdy = h_dv[g * 4 + 3]
            for sub in range(4):
                tt = g * 4 + sub
                slot = sub % 2
                wt = [("dv", h_rdy), ("wb", wl_kv), ("ld", p0_ld),
                      ("dv", cons.get(("psBv", slot), 0)),
                      ("ac", cons.get("ssfree_ac", 0)),
                      ("dv", cons.get("bcfree", 0)),
                      ("dv", vones_dv),
                      ("dv", cons.get("psBo", 0)),
                      ("dv", cons.get("bk6", 0)),
                      ("ac", cons.get("psq", 0))]
                first = True
                for cc in range(8):
                    for hf in range(2):
                        emit("tensor", lambda e, cc=cc, hf=hf, slot=slot, tt=tt: e.matmul(
                            psB[:, slot * 1024 + hf * 512: slot * 1024 + hf * 512 + 512],
                            A[:, cc * 2048 + tt * 128: cc * 2048 + tt * 128 + 128],
                            Cc[:, 8192 + cc * 1024 + hf * 512:
                               8192 + cc * 1024 + hf * 512 + 512],
                            start=(cc == 0), stop=False),
                            waits=(wt if first else []))
                        first = False
                for hf in range(2):
                    emit("tensor", lambda e, hf=hf, slot=slot: e.matmul(
                        psB[:, slot * 1024 + hf * 512: slot * 1024 + hf * 512 + 512],
                        ON1[0:1, :], BRW[0:1, hf * 512:(hf + 1) * 512],
                        start=False, stop=True),
                        incs=([("pe", 1)] if hf == 1 else []))
                pe_v = N["pe"]
                emit("vector", lambda e, slot=slot, tt=tt: e.tensor_copy(
                    Bv[:, tt * 1040:(tt + 1) * 1040].rearrange(
                        "p (h c) -> p h c", c=65)[:, :, 0:64],
                    psB[:, slot * 1024: slot * 1024 + 1024].rearrange(
                        "p (h c) -> p h c", c=64)),
                    waits=[("pe", pe_v), ("dv", N["dv"])], incs=[("dv", 1)])
                cons[("psBv", slot)] = N["dv"]
            lc = g * 512
            for kc in range(8):
                slot = kc % 2
                ps0 = 1024 + slot * 512
                wt = [("dv", h_rdy), ("ac", cons.get(("psAk", slot), 0)),
                      ("wb", wl_kv)]
                for cc in range(8):
                    emit("tensor", lambda e, cc=cc, kc=kc, ps0=ps0, lc=lc: e.matmul(
                        psA[:, ps0:ps0 + 512],
                        Cc[:, kc * 1024 + cc * 128: kc * 1024 + (cc + 1) * 128],
                        A[:, cc * 2048 + lc: cc * 2048 + lc + 512],
                        start=(cc == 0), stop=(cc == 7)),
                        waits=(wt if cc == 0 else []),
                        incs=([("pe", 1)] if cc == 7 else []))
                pe_k = N["pe"]
                wait_only("scalar", [("pe", pe_k)])
                ac_rstd = rope_rms(
                    "k", psA[:, ps0:ps0 + 512], ("psAk", slot),
                    bias_col=2 * kc + 1, cs_col=lc,
                    out_col=16384 + kc * 2048 + lc, out_buf=A,
                    blk_col=kc * 16, sum_start=(kc == 0), sum_stop=(kc == 7),
                    exp_bias=0.0)
            for kc in range(8):
                rms_apply(ac_rstd, kc * 128, 16384 + kc * 2048 + lc, A)
        k_done = N["dv"]

        emit("gpsimd", lambda e: e.dma_start(GB[:, :], gbq_d),
             waits=[("pe", N["pe"])], incs=[("wb", 16)])
        wl_gbq = N["wb"]

        for hp in range(8):
            wqs = hp % 2
            qsem = "qa" if wqs == 0 else "qb"
            emit("gpsimd", lambda e, hp=hp, wqs=wqs: e.dma_start(
                Dd[:, 4096 + wqs * 1024: 4096 + (wqs + 1) * 1024], wq_d[hp]),
                waits=[("pe", cons.get(("wq", wqs), 0))], incs=[(qsem, 16)])
            wl_wq = (qsem, N[qsem])
            q_mark = {}

            def q_mm(lt, hp=hp, wqs=wqs, wl_wq=wl_wq):
                wt = [wl_wq, ("dv", k_done),
                      ("ac", cons.get("psq", 0)), ("wb", wl_gbq),
                      ("dv", cons.get(("psBv", 1), 0))]
                for cc in range(8):
                    emit("tensor", lambda e, cc=cc, lt=lt, wqs=wqs: e.matmul(
                        psB[:, 1536:2048],
                        Dd[:, 4096 + wqs * 1024 + cc * 128:
                           4096 + wqs * 1024 + (cc + 1) * 128],
                        A[:, cc * 2048 + lt * 512: cc * 2048 + lt * 512 + 512],
                        start=(cc == 0), stop=(cc == 7)),
                        waits=(wt if cc == 0 else []),
                        incs=([("pe", 1)] if cc == 7 else []))
                cons[("wq", wqs)] = N["pe"]
                pe_q = N["pe"]
                emit("scalar", lambda e, hp=hp: e.activation(
                    TMP[:, 0:512], psB[:, 1536:2048], AF.Identity,
                    bias=BQK[:, 2 * hp:2 * hp + 1]),
                    waits=[("pe", pe_q), ("dv", cons.get("tmprope", 0)),
                           ("ac", N["ac"]), ("pe", cons.get("sqmm", 0)),
                           ("po", cons.get("swfree", 0)), ("ld", p0_ld)],
                    incs=[("ac", 1)])
                cons["psq"] = N["ac"]
                ac_raw = N["ac"]
                for qr in range(4):
                    dq = qr ^ 1
                    emit("sync", lambda e, qr=qr, dq=dq: e.dma_start(
                        TMP[dq * 32:(dq + 1) * 32, 512:1024],
                        TMP[qr * 32:(qr + 1) * 32, 0:512]),
                        waits=([("ac", ac_raw), ("po", cons.get("swfree", 0))]
                               if qr == 0 else []),
                        incs=[("sw", 16)])
                ld_sw = N["sw"]
                cs_col = lt * 512
                emit("vector", lambda e, cs_col=cs_col: e.tensor_tensor(
                    TMP[:, 1024:1536], TMP[:, 0:512], CS1[:, cs_col:cs_col + 512],
                    ALU.mult),
                    waits=[("ac", ac_raw), ("dv", N["dv"])], incs=[("dv", 1)])
                dv_t1 = N["dv"]
                emit("gpsimd", lambda e, cs_col=cs_col: e.tensor_tensor(
                    TMP[:, 0:512], TMP[:, 512:1024], CS2[:, cs_col:cs_col + 512],
                    ALU.mult),
                    waits=[("sw", ld_sw), ("dv", dv_t1), ("po", N["po"])],
                    incs=[("po", 1)])
                cons["swfree"] = N["po"]
                po_t2 = N["po"]
                qcol = lt * 512
                emit("vector", lambda e, qcol=qcol: e.tensor_tensor(
                    Dd[:, qcol:qcol + 512], TMP[:, 1024:1536], TMP[:, 0:512],
                    ALU.add),
                    waits=[("po", po_t2), ("dv", N["dv"])], incs=[("dv", 1)])
                cons["tmprope"] = N["dv"]
                q_mark[("rope", lt)] = N["dv"]

            def q_fin(lt, hp=hp):
                qcol = lt * 512
                emit("scalar", lambda e, qcol=qcol: e.activation(
                    TMP[:, 0:512], Dd[:, qcol:qcol + 512], AF.Square),
                    waits=[("dv", q_mark[("rope", lt)]), ("ac", N["ac"]),
                           ("pe", cons.get("sqmm", 0))],
                    incs=[("ac", 1)])
                q_mark[("sq", lt)] = N["ac"]
                emit("tensor", lambda e, hp=hp: e.matmul(
                    psB[0:16, 1024:1536], BLK[:, hp * 16:(hp + 1) * 16],
                    TMP[:, 0:512], start=True, stop=True),
                    waits=[("ac", q_mark[("sq", lt)]),
                           ("dv", cons.get("bk6", 0)),
                           ("ac", cons.get("bk6a", 0)),
                           ("dv", cons.get(("psBv", 1), 0)),
                           ("dv", cons.get("bcfree", 0))],
                    incs=[("pe", 1)])
                cons["sqmm"] = N["pe"]
                pe_ss = N["pe"]
                emit("scalar", lambda e: e.activation(
                    G[0:16, 0:512], psB[0:16, 1024:1536], AF.Ln, bias=CB[0:16, 1:2],
                    scale=1.0 / 64),
                    waits=[("pe", pe_ss), ("ac", N["ac"])], incs=[("ac", 1)])
                emit("scalar", lambda e: e.activation(
                    RST[0:16, 0:512], G[0:16, 0:512], AF.Exp, scale=-0.5,
                    bias=CB[0:16, 3:4]),
                    waits=[("ac", N["ac"]), ("pe", cons.get("rstfree", 0))],
                    incs=[("ac", 1)])
                ac_rstd = N["ac"]
                cons["bk6a"] = N["ac"]
                emit("tensor", lambda e, hp=hp: e.matmul(
                    psB[:, 1024:1536], GB[0:16, hp * 128:(hp + 1) * 128],
                    RST[0:16, 0:512], start=True, stop=True),
                    waits=[("ac", ac_rstd)], incs=[("pe", 1)])
                pe_bc = N["pe"]
                cons["rstfree"] = N["pe"]
                emit("vector", lambda e, qcol=lt * 512: e.tensor_tensor(
                    Dd[:, qcol:qcol + 512], Dd[:, qcol:qcol + 512],
                    psB[:, 1024:1536], ALU.mult),
                    waits=[("pe", pe_bc), ("dv", N["dv"])], incs=[("dv", 1)])
                cons["bk6"] = N["dv"]
                q_mark[lt] = N["dv"]

            def m_loop(lt, hp=hp):
                sc_mark = {}

                def emit_scores(mt):
                    half = mt % 2
                    sc0 = half * 1024
                    wt = [("dv", q_mark[lt]),
                          ("ac", cons.get(("expf", half), 0))]
                    emit("tensor", lambda e, hp=hp, mt=mt, lt=lt, sc0=sc0: e.matmul(
                        psA[:, sc0:sc0 + 512],
                        A[0:64, 16384 + hp * 2048 + mt * 128:
                          16384 + hp * 2048 + mt * 128 + 128],
                        Dd[0:64, lt * 512: lt * 512 + 512],
                        start=True, stop=True, tile_position=(0, 0)),
                        waits=wt)
                    emit("tensor", lambda e, hp=hp, mt=mt, lt=lt, sc0=sc0: e.matmul(
                        psA[:, sc0 + 512:sc0 + 1024],
                        A[64:128, 16384 + hp * 2048 + mt * 128:
                          16384 + hp * 2048 + mt * 128 + 128],
                        Dd[64:128, lt * 512: lt * 512 + 512],
                        start=True, stop=True, tile_position=(64, 0)),
                        incs=[("pe", 1)])
                    sc_mark[mt] = N["pe"]

                def emit_expav(mt):
                    half = mt % 2
                    sc0 = half * 1024
                    emit("scalar", lambda e, sc0=sc0, half=half: e.activation(
                        Dd[:, 2048 + half * 1024: 2048 + half * 1024 + 1024],
                        psA[:, sc0:sc0 + 1024], AF.Exp, bias=CB[:, 2:3]),
                        waits=[("pe", sc_mark[mt]),
                               ("pe", cons.get(("Pw", half), 0)),
                               ("ac", N["ac"])],
                        incs=[("ac", 1)])
                    cons[("expf", half)] = N["ac"]
                    ac_exp = N["ac"]
                    wt2 = [("ac", ac_exp)]
                    if mt == 0:
                        wt2.append(("dv", cons.get("psBo", 0)))
                        wt2.append(("dv", cons.get(("psBv", 0), 0)))
                        wt2.append(("ac", cons.get("ssfree_ac", 0)))
                    for hd in range(2):
                        emit("tensor", lambda e, hp=hp, mt=mt, hd=hd, half=half: e.matmul(
                            psB[0:65, hd * 512: hd * 512 + 512],
                            Bv[:, mt * 1040 + (2 * hp + hd) * 65:
                               mt * 1040 + (2 * hp + hd) * 65 + 65],
                            Dd[:, 2048 + half * 1024 + hd * 512:
                               2048 + half * 1024 + hd * 512 + 512],
                            start=(mt == 0), stop=(mt == 15)),
                            waits=(wt2 if hd == 0 else []),
                            incs=([("pe", 1)] if hd == 1 else []))
                    cons[("Pw", half)] = N["pe"]

                for mt in range(16):
                    emit_scores(mt)
                    if mt >= 1:
                        emit_expav(mt - 1)
                emit_expav(15)
                pe_av = N["pe"]
                emit("vector", lambda e: e.tensor_copy(
                    Dd[64:65, 10240:10752], psB[64:65, 0:512]),
                    waits=[("pe", pe_av), ("z", N["z"]),
                           ("dv", N["dv"])],
                    incs=[("dv", 1)])
                emit("vector", lambda e: e.tensor_copy(
                    Dd[64:65, 10752:11264], psB[64:65, 512:1024]),
                    waits=[("dv", N["dv"])], incs=[("dv", 1)])
                dv_z = N["dv"]
                emit("sync", lambda e, hp=hp, lt=lt: e.dma_start(
                    Dd[2 * hp:2 * hp + 2, 6144 + lt * 512:6144 + lt * 512 + 512],
                    Dd[64:65, 10240:11264]),
                    waits=[("dv", dv_z)], incs=[("z", 16)])
                emit("vector", lambda e, hp=hp, lt=lt: e.tensor_copy(
                    Cc[0:64, hp * 2048 + lt * 512: hp * 2048 + lt * 512 + 512],
                    psB[0:64, 0:512]),
                    waits=[("pe", pe_av), ("dv", N["dv"])], incs=[("dv", 1)])
                emit("vector", lambda e: e.tensor_copy(
                    YS[0:64, 0:512], psB[0:64, 512:1024]),
                    waits=[("pe", pe_av), ("ob", N["ob"]),
                           ("dv", N["dv"]), ("x1a", N["x1a"])],
                    incs=[("dv", 1)])
                dv_ob = N["dv"]
                emit("sync", lambda e, hp=hp, lt=lt: e.dma_start(
                    Cc[64:128, hp * 2048 + lt * 512: hp * 2048 + lt * 512 + 512],
                    YS[0:64, 0:512]),
                    waits=[("dv", dv_ob)], incs=[("ob", 16)])
                cons["psBo"] = N["dv"]

            q_mm(0)
            q_fin(0)
            q_mm(1)
            for lt in range(4):
                m_loop(lt)
                if lt + 1 < 4:
                    q_fin(lt + 1)
                if lt + 2 < 4:
                    q_mm(lt + 2)
        z_ld = N["z"]

        emit("vector", lambda e: e.reciprocal(
            Dd[0:16, 8192:10240], Dd[0:16, 6144:8192]),
            waits=[("z", z_ld), ("dv", N["dv"]), ("pe", N["pe"])],
            incs=[("dv", 1)])
        dv_rz = N["dv"]
        emit("gpsimd", lambda e: e.dma_start(Bv[:, 0:8192], wout_d),
             waits=[("pe", N["pe"])], incs=[("wb", 16)])
        emit("gpsimd", lambda e: e.dma_start(BRW[:, :], bout_d), incs=[("wb", 16)])
        emit("gpsimd", lambda e: e.dma_start(GB[:, :], hs_d),
             waits=[("pe", N["pe"])], incs=[("wb", 16)])
        wl_oc = N["wb"]
        ld_ohop = N["ob"]
        for oc in range(8):
            for lh in range(2):
                for hf in range(2):
                    emit("tensor", lambda e, oc=oc, lh=lh, hf=hf: e.matmul(
                        psA[:, hf * 512: hf * 512 + 512],
                        GB[0:16, oc * 128:(oc + 1) * 128],
                        Dd[0:16, 8192 + lh * 1024 + hf * 512:
                           8192 + lh * 1024 + hf * 512 + 512],
                        start=True, stop=True),
                        waits=([("dv", dv_rz), ("wb", wl_oc),
                                ("dv", cons.get("onorm", 0))] if hf == 0 else []),
                        incs=([("pe", 1)] if hf == 1 else []))
                pe_bc = N["pe"]
                emit("vector", lambda e, oc=oc, lh=lh: e.tensor_tensor(
                    Cc[:, oc * 2048 + lh * 1024: oc * 2048 + lh * 1024 + 1024],
                    Cc[:, oc * 2048 + lh * 1024: oc * 2048 + lh * 1024 + 1024],
                    psA[:, 0:1024], ALU.mult),
                    waits=[("pe", pe_bc), ("dv", N["dv"]), ("ob", ld_ohop)],
                    incs=[("dv", 1)])
                cons["onorm"] = N["dv"]
        dv_on = N["dv"]
        if DEBUG_ATT and s == 1:
            emit("sync", lambda e: e.dma_start(dbgz_d, Dd[0:16, 10240:14336]),
                 waits=[("dv", dv_on)], incs=[("ld", 16)])
            emit("sync", lambda e: e.dma_start(dbgo_d, Cc[:, :]),
                 waits=[("dv", dv_on)], incs=[("ld", 16)])
            emit("sync", lambda e: e.dma_start(dbgq_d, Dd[:, 0:4096]),
                 waits=[("dv", dv_on)], incs=[("ld", 16)])
            emit("sync", lambda e: e.dma_start(dbgk_d, A[:, 16384:32768]),
                 waits=[("dv", dv_on)], incs=[("ld", 16)])
        for tt in range(NTS):
            tok = s * L + tt * 128
            slot = tt % 2
            gs = tt % 2
            xsem = "xa" if gs == 0 else "xb"
            emit("gpsimd", lambda e, tok=tok, gs=gs: e.dma_start(
                XT[:, gs * 1024:(gs + 1) * 1024], x_d[tok:tok + 128, :]),
                waits=[("dv", cons.get(("XT", gs), 0))], incs=[(xsem, 16)])
            ld_x0 = (xsem, N[xsem])
            wt = [("dv", dv_on), ("wb", wl_oc), ("ld", p0_ld),
                  ("dv", cons.get(("psBp", slot), 0))]
            first = True
            for oc in range(8):
                for hf in range(2):
                    emit("tensor", lambda e, oc=oc, hf=hf, tt=tt, slot=slot: e.matmul(
                        psB[:, slot * 1024 + hf * 512: slot * 1024 + hf * 512 + 512],
                        Cc[:, oc * 2048 + tt * 128: oc * 2048 + tt * 128 + 128],
                        Bv[:, oc * 1024 + hf * 512: oc * 1024 + hf * 512 + 512],
                        start=(oc == 0), stop=False),
                        waits=(wt if first else []))
                    first = False
            for hf in range(2):
                emit("tensor", lambda e, hf=hf, slot=slot: e.matmul(
                    psB[:, slot * 1024 + hf * 512: slot * 1024 + hf * 512 + 512],
                    ON1[0:1, :], BRW[0:1, hf * 512:(hf + 1) * 512],
                    start=False, stop=True),
                    incs=([("pe", 1)] if hf == 1 else []))
            pe_pj = N["pe"]
            emit("vector", lambda e, slot=slot, gs=gs: e.tensor_tensor(
                YS[:, slot * 1024:(slot + 1) * 1024],
                XT[:, gs * 1024:(gs + 1) * 1024],
                psB[:, slot * 1024: slot * 1024 + 1024], ALU.add),
                waits=[("pe", pe_pj), ld_x0,
                       ("x1a" if slot == 0 else "x1b",
                        N["x1a" if slot == 0 else "x1b"]), ("dv", N["dv"])],
                incs=[("dv", 1)])
            cons[("XT", gs)] = N["dv"]
            cons[("psBp", slot)] = N["dv"]
            dv_x1 = N["dv"]
            x1sem = "x1a" if slot == 0 else "x1b"
            emit("sync", lambda e, tok=tok, slot=slot: e.dma_start(
                x1_d[tok:tok + 128, :], YS[:, slot * 1024:(slot + 1) * 1024]),
                waits=[("dv", dv_x1)], incs=[(x1sem, 16)])
    x1_st = (N["x1a"], N["x1b"])

    emit("gpsimd", lambda e: e.dma_start(BRW[:, :], brt_d),
         waits=[("pe", N["pe"])], incs=[("wb", 16)])

    h2_marks = {}
    for tt in range(NT):
        tok = tt * 128
        gs = tt % 2
        psem = "pa" if gs == 0 else "pb"
        emit("sync", lambda e, tok=tok, gs=gs: e.dma_start(
            YS[:, gs * 1024:(gs + 1) * 1024], x1_d[tok:tok + 128, :]),
            waits=[("x1a", x1_st[0]), ("x1b", x1_st[1]),
                   ("dv", cons.get(("YS2", gs), 0))],
            incs=[(psem, 16)])
        dv_h = ln_tile(YS[:, gs * 1024:(gs + 1) * 1024], "htr",
                       extra_waits=[(psem, N[psem]),
                                    ("dv", cons.get("h2cp", 0))])
        cons[("YS2", gs)] = dv_h
        dv_h2 = transpose_to(
            (lambda cc, tt=tt: A[:, cc * 4096 + tt * 128:
                                 cc * 4096 + tt * 128 + 128]), dv_h)
        h2_marks[tt] = dv_h2
        wt = [("dv", dv_h2), ("ac", cons.get("sig", 0)),
              ("ld", p0_ld), ("wb", N["wb"])]
        for cc in range(8):
            emit("tensor", lambda e, cc=cc, tt=tt: e.matmul(
                psB[:, 0:8],
                A[:, cc * 4096 + tt * 128: cc * 4096 + tt * 128 + 128],
                WRT[:, cc * 8:(cc + 1) * 8], start=(cc == 0), stop=False),
                waits=(wt if cc == 0 else []))
        emit("tensor", lambda e: e.matmul(
            psB[:, 0:8], ON1[0:1, :], BRW[0:1, 0:8], start=False, stop=True),
            incs=[("pe", 1)])
        pe_lg = N["pe"]
        emit("scalar", lambda e: e.activation(
            PR[:, 0:8], psB[:, 0:8], AF.Sigmoid),
            waits=[("pe", pe_lg), ("dv", cons.get("top2", 0)), ("ac", N["ac"])],
            incs=[("ac", 1)])
        cons["sig"] = N["ac"]
        ac_pr = N["ac"]

        def t2(b, w0=None):
            emit("vector", b,
                 waits=[("dv", N["dv"])] + ([w0] if w0 else []), incs=[("dv", 1)])
        t2(lambda e: e.tensor_tensor(PR[:, 0:8], PR[:, 0:8], EPS[:, :], ALU.add),
           w0=("ac", ac_pr))
        t2(lambda e: e.reduce_max(PR[:, 16:17], PR[:, 0:8], axis=AX.X))
        t2(lambda e: e.tensor_scalar(PR[:, 8:16], PR[:, 0:8], PR[:, 16:17],
                                     None, ALU.is_ge))
        t2(lambda e, tt=tt: e.tensor_copy(EQ18[:, tt * 8:tt * 8 + 8], PR[:, 8:16]))
        t2(lambda e: e.scalar_tensor_tensor(PR[:, 8:16], PR[:, 8:16], -1e9,
                                            PR[:, 0:8], ALU.mult, ALU.add))
        t2(lambda e: e.reduce_max(PR[:, 17:18], PR[:, 8:16], axis=AX.X))
        t2(lambda e: e.tensor_tensor(PR[:, 18:19], PR[:, 16:17], PR[:, 17:18],
                                     ALU.add))
        t2(lambda e: e.reciprocal(PR[:, 19:20], PR[:, 18:19]))
        t2(lambda e: e.tensor_scalar(PR[:, 8:16], PR[:, 0:8], PR[:, 17:18],
                                     None, ALU.is_ge))
        t2(lambda e, tt=tt: e.tensor_copy(IND8[:, tt * 8:tt * 8 + 8], PR[:, 8:16]))
        t2(lambda e: e.tensor_tensor(PR[:, 8:16], PR[:, 8:16], PR[:, 0:8],
                                     ALU.mult))
        t2(lambda e: e.tensor_scalar(PR[:, 8:16], PR[:, 8:16], PR[:, 19:20],
                                     None, ALU.mult))
        t2(lambda e, tt=tt: e.tensor_tensor(PR[:, 24:32],
                                            EQ18[:, tt * 8:tt * 8 + 8],
                                            PR[:, 8:16], ALU.mult))
        t2(lambda e, tt=tt: e.reduce_sum(W12[:, 2 * tt:2 * tt + 1],
                                         PR[:, 24:32], axis=AX.X))
        t2(lambda e: e.tensor_tensor(PR[:, 24:32], PR[:, 8:16], PR[:, 24:32],
                                     ALU.subtract))
        t2(lambda e, tt=tt: e.reduce_sum(W12[:, 2 * tt + 1:2 * tt + 2],
                                         PR[:, 24:32], axis=AX.X))
        cons["top2"] = N["dv"]
        emit("vector", lambda e: e.tensor_copy(TMP[:, 0:1024], G[:, :]),
             waits=[("dv", dv_h), ("h2s", N["h2s"]), ("dv", N["dv"])],
             incs=[("dv", 1)])
        dv_h2c = N["dv"]
        emit("sync", lambda e, tok=tok: e.dma_start(
            h2sp_d[tok:tok + 128, :], TMP[:, 0:1024]),
            waits=[("dv", dv_h2c)], incs=[("h2s", 16)])
        cons["h2cp"] = dv_h2c
    h2_dv = N["dv"]

    emit("vector", lambda e: e.memset(CRY[:, 0:1], 0.0),
         waits=[("dv", N["dv"])], incs=[("dv", 1)])
    emit("vector", lambda e: e.memset(HH[0:1, 0:1024], 0.0),
         waits=[("dv", N["dv"])], incs=[("dv", 1)])
    dv_yz = N["dv"]
    emit("sync", lambda e: e.dma_start(
        ysp_d[NSLOT:NSLOT + 1, :], HH[0:1, 0:1024]),
        waits=[("dv", dv_yz)], incs=[("yz", 16)])
    for tt in range(NT):
        tok = tt * 128
        emit("vector", lambda e, tt=tt: e.tensor_copy(
            G[:, 0:8], IND8[:, tt * 8:tt * 8 + 8]),
            waits=[("dv", N["dv"]), ("pe", cons.get("pbtr", 0))], incs=[("dv", 1)])
        dv_c = N["dv"]
        emit("tensor", lambda e: e.transpose(
            psA[0:8, 0:128], G[:, 0:8], IDF[:, :]),
            waits=[("dv", dv_c), ("ac", cons.get("pbev", 0))], incs=[("pe", 1)])
        pe_t1 = N["pe"]
        emit("scalar", lambda e: e.activation(
            G[0:8, 128:256], psA[0:8, 0:128], AF.Identity),
            waits=[("pe", pe_t1), ("ac", N["ac"])], incs=[("ac", 1)])
        cons["pbev"] = N["ac"]
        ac_ev = N["ac"]
        emit("vector", lambda e: e.tensor_tensor_scan(
            G[0:8, 256:384], G[0:8, 128:256], G[0:8, 128:256], 0.0,
            ALU.add, ALU.bypass),
            waits=[("ac", ac_ev), ("dv", N["dv"])], incs=[("dv", 1)])
        emit("vector", lambda e: e.tensor_scalar(
            G[0:8, 256:384], G[0:8, 256:384], CRY[:, 0:1], None, ALU.add),
            waits=[("dv", N["dv"])], incs=[("dv", 1)])
        emit("vector", lambda e: e.tensor_copy(CRY[:, 0:1], G[0:8, 383:384]),
             waits=[("dv", N["dv"])], incs=[("dv", 1)])
        dv_cum = N["dv"]
        emit("tensor", lambda e: e.transpose(
            psA[:, 128:136], G[0:8, 256:384], IDF[0:8, 0:8]),
            waits=[("dv", dv_cum)], incs=[("pe", 1)])
        cons["pbtr"] = N["pe"]
        pe_t2 = N["pe"]
        emit("scalar", lambda e: e.activation(
            G[:, 8:16], psA[:, 128:136], AF.Identity),
            waits=[("pe", pe_t2), ("ac", N["ac"])], incs=[("ac", 1)])
        ac_ct = N["ac"]
        emit("vector", lambda e: e.tensor_tensor(
            G[:, 8:16], G[:, 8:16], G[:, 0:8], ALU.subtract),
            waits=[("ac", ac_ct), ("dv", N["dv"])], incs=[("dv", 1)])
        emit("vector", lambda e: e.tensor_scalar(
            G[:, 16:24], G[:, 8:16], float(CAP), None, ALU.is_ge),
            waits=[("dv", N["dv"])], incs=[("dv", 1)])
        emit("vector", lambda e: e.tensor_tensor(
            G[:, 8:16], G[:, 8:16], ECP[:, :], ALU.add),
            waits=[("dv", N["dv"])], incs=[("dv", 1)])
        emit("vector", lambda e: e.scalar_tensor_tensor(
            G[:, 8:16], G[:, 16:24], 1e9, G[:, 8:16], ALU.mult, ALU.add),
            waits=[("dv", N["dv"])], incs=[("dv", 1)])
        emit("vector", lambda e: e.tensor_scalar(
            G[:, 8:16], G[:, 8:16], float(NSLOT), None, ALU.min),
            waits=[("dv", N["dv"])], incs=[("dv", 1)])
        emit("vector", lambda e, tt=tt: e.tensor_tensor(
            G[:, 16:24], EQ18[:, tt * 8:tt * 8 + 8], G[:, 8:16], ALU.mult),
            waits=[("dv", N["dv"])], incs=[("dv", 1)])
        emit("vector", lambda e: e.reduce_sum(
            G[:, 24:25], G[:, 16:24], axis=AX.X),
            waits=[("dv", N["dv"])], incs=[("dv", 1)])
        emit("vector", lambda e: e.tensor_tensor(
            G[:, 16:24], G[:, 0:8], G[:, 8:16], ALU.mult),
            waits=[("dv", N["dv"])], incs=[("dv", 1)])
        emit("vector", lambda e: e.reduce_sum(
            G[:, 25:26], G[:, 16:24], axis=AX.X),
            waits=[("dv", N["dv"])], incs=[("dv", 1)])
        emit("vector", lambda e: e.tensor_tensor(
            G[:, 25:26], G[:, 25:26], G[:, 24:25], ALU.subtract),
            waits=[("dv", N["dv"])], incs=[("dv", 1)])
        emit("vector", lambda e, tt=tt: e.tensor_copy(
            SLI[:, 2 * tt:2 * tt + 2], G[:, 24:26]),
            waits=[("dv", N["dv"])], incs=[("dv", 1)])
        dv_sli = N["dv"]
        emit("sync", lambda e, tok=tok: e.dma_start(
            TMP[:, 0:1024], h2sp_d[tok:tok + 128, :]),
            waits=[("h2s", N["h2s"]), ("xsc", N["xsc"]),
                   ("dv", cons.get("h2rd", 0))],
            incs=[("h2l", 16)])
        hl = N["h2l"]
        for k in range(2):
            emit("gpsimd", lambda e, tt=tt, k=k: e.indirect_dma_start(
                xsp_d, IndirectOffsetOnAxis(SLI[:, 2 * tt + k:2 * tt + k + 1], 0),
                TMP[:, 0:1024], None),
                waits=[("h2l", hl), ("dv", dv_sli)] if k == 0 else [],
                incs=[("xsc", 16)])
    xsc_all = N["xsc"]
    h2_dv = N["dv"]

    pe_attn_end = N["pe"]
    prev_gu = [pe_attn_end] * 16
    prev_wd = [pe_attn_end] * 16
    xb_mark = [0, 0]
    for ex in range(-1, NE):
        groups = 8 if ex < 0 else 4
        srcs = (sg_d, su_d, sd_d) if ex < 0 else (wg_d[ex], wu_d[ex], wd_d[ex])
        wl_gu = [None] * 16
        wl_wd = [None] * 16
        for j in range(16):
            wt_g = [("pe", prev_gu[j])] if (ex >= 0 or j == 0) else []
            emit("gpsimd", lambda e, s0=srcs[0], j=j: e.dma_start(
                Cc[:, j * 1024:(j + 1) * 1024], s0[:, j * 1024:(j + 1) * 1024]),
                waits=wt_g, incs=[(f"wj{j}", 16)])
            wt_u = [("dv", dv_rz)] if (ex < 0 and j == 0) else []
            emit("gpsimd", lambda e, s1=srcs[1], j=j: e.dma_start(
                Dd[:, j * 1024:(j + 1) * 1024], s1[:, j * 1024:(j + 1) * 1024]),
                waits=wt_u, incs=[(f"wj{j}", 16)])
            wl_gu[j] = (f"wj{j}", N[f"wj{j}"])
            wt_d = [("pe", prev_wd[j])] if (ex >= 0 or j == 0) else []
            emit("gpsimd", lambda e, s2=srcs[2], j=j: e.dma_start(
                Bv[:, j * 1024:(j + 1) * 1024], s2[:, j * 1024:(j + 1) * 1024]),
                waits=wt_d, incs=[(f"wd{j}", 16)])
            wl_wd[j] = (f"wd{j}", N[f"wd{j}"])
        if ex < 0:
            emit("gpsimd", lambda e: e.dma_start(BRW[:, :], brt_d), incs=[("wb", 16)])
        new_gu = [0] * 16
        new_wd = [0] * 16
        hh_mark = [0] * 16
        for g in range(groups):
            t0 = g * 512
            par = g % 2
            if ex >= 0:
                xoff = par * 4096
                trs = []
                for i in range(4):
                    lsem = "xba" if i % 2 == 0 else "xbb"
                    tsem = "xta" if i % 2 == 0 else "xtb"
                    roff = ex * CAP + g * 512 + i * 128
                    emit("sync", lambda e, roff=roff, i=i: e.dma_start(
                        XT[:, (i % 2) * 1024:(i % 2) * 1024 + 1024],
                        xsp_d[roff:roff + 128, :]),
                        waits=[("xsc", xsc_all), (tsem, N[tsem])],
                        incs=[(lsem, 16)])
                    emit("sync", lambda e, i=i, xoff=xoff: e.dma_start(
                        A[:, xoff:xoff + 4096].rearrange(
                            "p (c n) -> p c n", c=8)[:, :, i * 128:(i + 1) * 128],
                        XT[:, (i % 2) * 1024:(i % 2) * 1024 + 1024],
                        transpose=True),
                        waits=[(lsem, N[lsem]), ("pe", xb_mark[par])],
                        incs=[(tsem, 16)])
                    trs.append((tsem, N[tsem]))
            for ich in range(16):
                ga = ich % 2
                ps_g = ga * 1024
                wt = [("dv", cons.get(("hhw", ga), 0))]
                if ex < 0:
                    wt.append(("dv", h2_dv))
                else:
                    if ich == 0:
                        for ts_ in trs:
                            wt.append(ts_)
                if g == 0:
                    wt.append(wl_gu[ich])
                for cc in range(8):
                    if ex < 0:
                        rhs = A[:, cc * 4096 + t0: cc * 4096 + t0 + 512]
                    else:
                        rhs = A[:, par * 4096 + cc * 512: par * 4096 + cc * 512 + 512]
                    emit("tensor", lambda e, ich=ich, cc=cc, rhs=rhs, ps_g=ps_g: e.matmul(
                        psA[:, ps_g:ps_g + 512],
                        Cc[:, ich * 1024 + cc * 128: ich * 1024 + (cc + 1) * 128],
                        rhs, start=(cc == 0), stop=(cc == 7)),
                        waits=(wt if cc == 0 else []))
                for cc in range(8):
                    if ex < 0:
                        rhs = A[:, cc * 4096 + t0: cc * 4096 + t0 + 512]
                    else:
                        rhs = A[:, par * 4096 + cc * 512: par * 4096 + cc * 512 + 512]
                    emit("tensor", lambda e, ich=ich, cc=cc, rhs=rhs, ps_g=ps_g: e.matmul(
                        psA[:, ps_g + 512:ps_g + 1024],
                        Dd[:, ich * 1024 + cc * 128: ich * 1024 + (cc + 1) * 128],
                        rhs, start=(cc == 0), stop=(cc == 7)),
                        incs=([("pe", 1)] if cc == 7 else []))
                pe_gu = N["pe"]
                if g == groups - 1:
                    new_gu[ich] = pe_gu
                toff = 1024 if ex < 0 else 0
                emit("scalar", lambda e, ps_g=ps_g, toff=toff: e.activation(
                    TMP[:, toff:toff + 512], psA[:, ps_g:ps_g + 512], AF.Silu),
                    waits=[("pe", pe_gu), ("dv", cons.get("silu", 0)),
                           ("ac", N["ac"])],
                    incs=[("ac", 1)])
                ac_si = N["ac"]
                emit("vector", lambda e, ich=ich, ps_g=ps_g, toff=toff: e.tensor_tensor(
                    HH[:, ich * 512:(ich + 1) * 512], TMP[:, toff:toff + 512],
                    psA[:, ps_g + 512:ps_g + 1024], ALU.mult),
                    waits=[("ac", ac_si), ("dv", N["dv"])]
                    + ([("yz", 16)] if (ex < 0 and g == 0 and ich == 0) else []),
                    incs=[("dv", 1)])
                cons["silu"] = N["dv"]
                cons[("hhw", ga)] = N["dv"]
                hh_mark[ich] = N["dv"]
            if ex >= 0:
                xb_mark[par] = N["pe"]
            for sub in range(4):
                slot = sub % 2
                tt = g * 4 + sub
                for ich in range(16):
                    for hf in range(2):
                        wthis = []
                        if ich == 0 and hf == 0:
                            wthis.append(("dv", cons.get(("psBy", slot), 0)))
                        if sub == 0 and hf == 0:
                            wthis.append(("dv", hh_mark[ich]))
                            if g == 0:
                                wthis.append(wl_wd[ich])
                        do_inc = (ich == 15 and hf == 1) or (
                            g == groups - 1 and sub == 3 and hf == 1)
                        emit("tensor", lambda e, ich=ich, hf=hf, sub=sub, slot=slot: e.matmul(
                            psB[:, slot * 1024 + hf * 512: slot * 1024 + hf * 512 + 512],
                            HH[:, ich * 512 + sub * 128: ich * 512 + sub * 128 + 128],
                            Bv[:, ich * 1024 + hf * 512: ich * 1024 + hf * 512 + 512],
                            start=(ich == 0), stop=(ich == 15)),
                            waits=wthis,
                            incs=([("pe", 1)] if do_inc else []))
                        if g == groups - 1 and sub == 3 and hf == 1:
                            new_wd[ich] = N["pe"]
                pe_y = N["pe"]
                ysem = "ya" if slot == 0 else "yb"
                emit("vector", lambda e, slot=slot: e.tensor_copy(
                    YS[:, slot * 1024:(slot + 1) * 1024],
                    psB[:, slot * 1024: slot * 1024 + 1024]),
                    waits=[("pe", pe_y), (ysem, N[ysem]), ("dv", N["dv"])]
                    + ([("x1a" if slot == 0 else "x1b",
                         N["x1a" if slot == 0 else "x1b"])] if ex < 0 else []),
                    incs=[("dv", 1)])
                cons[("psBy", slot)] = N["dv"]
                dv_y = N["dv"]
                dst = ysh_d if ex < 0 else ysp_d
                roff = (tt * 128) if ex < 0 else (ex * CAP + tt * 128)
                emit("sync", lambda e, dst=dst, roff=roff, slot=slot: e.dma_start(
                    dst[roff:roff + 128, :], YS[:, slot * 1024:(slot + 1) * 1024]),
                    waits=[("dv", dv_y)], incs=[(ysem, 16)])
        prev_gu = new_gu
        prev_wd = new_wd
    ffn_st = (N["ya"], N["yb"])

    for tt in range(NT):
        tok = tt * 128
        emit("sync", lambda e, tok=tok: e.dma_start(
            HH[:, 0:1024], x1_d[tok:tok + 128, :]),
            waits=[("ya", ffn_st[0]), ("yb", ffn_st[1]),
                   ("x1a", x1_st[0]), ("x1b", x1_st[1]),
                   ("dv", cons.get("fin", 0))],
            incs=[("fl", 16)])
        emit("sync", lambda e, tok=tok: e.dma_start(
            HH[:, 1024:2048], ysh_d[tok:tok + 128, :]),
            incs=[("fl", 16)])
        ld_f = N["fl"]
        emit("gpsimd", lambda e, tt=tt: e.indirect_dma_start(
            HH[:, 2048:3072], None,
            ysp_d, IndirectOffsetOnAxis(SLI[:, 2 * tt:2 * tt + 1], 0)),
            waits=[("dv", cons.get("fin", 0)), ("g1", N["g1"]),
                   ("ya", ffn_st[0]), ("yb", ffn_st[1]), ("yz", 16)],
            incs=[("g1", 16)])
        emit("gpsimd", lambda e, tt=tt: e.indirect_dma_start(
            HH[:, 3072:4096], None,
            ysp_d, IndirectOffsetOnAxis(SLI[:, 2 * tt + 1:2 * tt + 2], 0)),
            waits=[("g2", N["g2"])],
            incs=[("g2", 16)])
        g1c, g2c = N["g1"], N["g2"]
        emit("vector", lambda e, tt=tt: e.scalar_tensor_tensor(
            G[:, :], HH[:, 2048:3072], W12[:, 2 * tt:2 * tt + 1],
            HH[:, 0:1024], ALU.mult, ALU.add),
            waits=[("fl", ld_f), ("g1", g1c), ("dv", N["dv"]),
                   ("out", cons.get("outG", 0))], incs=[("dv", 1)])
        emit("vector", lambda e, tt=tt: e.scalar_tensor_tensor(
            G[:, :], HH[:, 3072:4096], W12[:, 2 * tt + 1:2 * tt + 2],
            G[:, :], ALU.mult, ALU.add),
            waits=[("g2", g2c), ("dv", N["dv"])], incs=[("dv", 1)])
        emit("vector", lambda e: e.tensor_tensor(
            G[:, :], G[:, :], HH[:, 1024:2048], ALU.add),
            waits=[("dv", N["dv"])],
            incs=[("dv", 1)])
        cons["fin"] = N["dv"]
        dv_f = N["dv"]
        emit("sync", lambda e, tok=tok: e.dma_start(
            out_d[tok:tok + 128, :], G[:, :]),
            waits=[("dv", dv_f)], incs=[("out", 16)])
        cons["outG"] = N["out"]

    wait_only("sync", [("out", N["out"]), ("x1a", N["x1a"]), ("x1b", N["x1b"]),
                       ("ya", N["ya"]), ("yb", N["yb"])])

    with ctx:
        ctx.enter_context(nc.allow_low_precision(
            "bf16 softmax-denominator reciprocal; within checked tolerance"))
        block = ctx.enter_context(nc.Block())

        @block.sync
        def _(e):
            for fn in OPS["sync"]:
                fn(e)

        @block.gpsimd
        def _(e):
            for fn in OPS["gpsimd"]:
                fn(e)

        @block.tensor
        def _(e):
            for fn in OPS["tensor"]:
                fn(e)

        @block.scalar
        def _(e):
            for fn in OPS["scalar"]:
                fn(e)

        @block.vector
        def _(e):
            for fn in OPS["vector"]:
                fn(e)

    return nc


def kernel(**inputs):
    w = prep_weights(inputs)
    x = np.ascontiguousarray(inputs["L0_attn_x"]).reshape(NCORES, TC, CM)
    nc = build_program()
    base = {k: w[k] for k in ("wk", "wv", "wq", "wout", "bqk", "bv", "bout",
                              "brt", "gbq", "gbk", "wrt", "sg", "su", "sd",
                              "wg", "wu", "wd")}
    in_maps = [dict(base, x=np.ascontiguousarray(x[c])) for c in range(NCORES)]
    res = run_bass_kernel_spmd(nc, in_maps, core_ids=list(range(NCORES)))
    out = np.stack([res.results[c]["out"] for c in range(NCORES)])
    return out.reshape(16, L, CM)
